# version 11
# baseline (speedup 1.0000x reference)
import numpy as np
import ml_dtypes
import concourse.bass as bass
import concourse.mybir as mybir
from concourse.bass_utils import run_bass_kernel_spmd
from contextlib import ExitStack

F32 = mybir.dt.float32
BF16 = mybir.dt.bfloat16
AF = mybir.ActivationFunctionType
ALU = mybir.AluOpType
AX = mybir.AxisListType
bfloat16 = ml_dtypes.bfloat16

D = 2048
DT = 16
TOKF = 2048
TOKO = 1024
NSLOT = 8
NH = 16
NE = 16
I_FF = 1408
NIT = 11
SHI = 2816
SCALE_ATTN = float(192.0 ** -0.5)
EPS_LN = 1e-5
EPS_RMS2 = 1.1920929e-07
EPS_LN1 = 1e-6
EPS_LN2 = 1e-6

ENGS = ("sync", "gpsimd", "tensor", "scalar", "vector")
P4A_JMAX = 16
P4A_NOMASK = False
P4A_MASKVAR = 0
P4A_ONECHUNK = False
P4A_ONEBUF = False


class Plan:
    def __init__(self, K):
        self.K = K
        self.streams = {e: [] for e in ENGS}
        eng_sems = ("pe", "act", "dve", "gp")
        self.guar = {e: {s: v for s, v in K.guar.items() if s in eng_sems}
                     for e in ENGS}
        self._seen = set()

    def op(self, eng, fn, waits=(), inc=None, inc_n=1):
        K = self.K
        if eng not in self._seen and getattr(K, "const_waits", None):
            waits = list(waits) + K.const_waits
        self._seen.add(eng)
        incs = []
        if inc is not None:
            K.cnt[inc] += inc_n
            incs = [(K.sems[inc], inc_n)]
        wl = []
        g = self.guar[eng]
        for s, v in waits:
            if v > g.get(s, 0):
                wl.append((K.sems[s], v))
                g[s] = v

        def emit(engine, wl=wl, fn=fn, incs=incs):
            for sem, v in wl:
                engine.wait_ge(sem, v)
            inst = fn(engine)
            for sem, n in incs:
                inst.then_inc(sem, n)

        self.streams[eng].append(emit)
        return K.cnt[inc] if inc is not None else None

    def dma(self, eng, out, in_, sem, waits=()):
        return self.op(eng, lambda e, o=out, i=in_: e.dma_start(out=o, in_=i),
                       waits=waits, inc=sem, inc_n=16)

    def run(self, nc, name):
        with nc.Block(name) as block:
            for eng_name in ENGS:
                ops = self.streams[eng_name]
                if not ops:
                    continue

                def section(engine, ops=ops):
                    for o in ops:
                        o(engine)

                getattr(block, eng_name)(section)
        self.K.guar = dict(self.K.cnt)


class KCtx:
    def __init__(self, nc, stack):
        self.nc = nc
        self.sems = {}
        self.cnt = {}
        self.guar = {}
        for n in ["pe", "act", "dve", "gp", "dio", "dwA", "dwB",
                  "kA", "kB", "kC", "vA", "vB", "vC", "sA", "sB"]:
            self.sems[n] = stack.enter_context(nc.semaphore(n))
            self.cnt[n] = 0

    def at(self, name, shape, dtype, off):
        return self.nc.alloc_sbuf_tensor_at(name, shape, dtype, offset=off)


def ap2(t, offset, dims):
    h = t.tensor if isinstance(t, bass.AP) else t
    return bass.AP(h, offset, [list(d) for d in dims])


def build(debug=False, upto=None):
    nc = bass.Bass("TRN2", target_bir_lowering=False)

    def din(name, shape, dt=F32):
        return nc.dram_tensor(name, list(shape), dt, kind="ExternalInput").ap()

    xoT = din("xoT", (D, TOKO))
    xfT = din("xfT", (D, TOKF))
    wq_t = din("wq_t", (D, D))
    wk_t = din("wk_t", (D, D))
    wv_t = din("wv_t", (D, D))
    wg_t = din("wg_t", (D, D))
    wp_t = din("wp_t", (D, D))
    qw_t = din("qw_t", (D, D))
    kvw_t = din("kvw_t", (D, 2 * D))
    ow_t = din("ow_t", (D, D))
    gw_t = din("gw_t", (D, NE))
    w1_t = din("w1_t", (NE, D, 2 * I_FF))
    w2_t = din("w2_t", (NE, I_FF, D))
    shgu_t = din("shgu_t", (D, 2 * SHI))
    shdn_t = din("shdn_t", (SHI, D))
    ln1_r = din("ln1_r", (128, DT))
    ln2_r = din("ln2_r", (128, DT))
    la_m0w = din("la_m0w", (128, 512), BF16)
    la_m1w = din("la_m1w", (128, 512), BF16)
    ff_w512 = din("ff_w512", (128, 512), BF16)
    ff_w256 = din("ff_w256", (128, 256), BF16)
    identb = din("identb", (128, 128), BF16)
    identf = din("identf", (128, 128), F32)
    ones_colf = din("ones_colf", (128, 1), F32)
    ones_colb = din("ones_colb", (128, 1), BF16)
    ones_rowb = din("ones_rowb", (1, 128), BF16)
    ones_rowf = din("ones_rowf", (1, 128), F32)
    epsv = din("epsv", (128, 4), F32)

    outT = nc.dram_tensor("outT", [D, TOKO], F32, kind="ExternalOutput").ap()

    skind = "ExternalOutput" if debug else "Internal"
    laK = nc.dram_tensor("laK", [D, TOKF], BF16, kind=skind).ap()
    laV = nc.dram_tensor("laV", [TOKF, D], BF16, kind=skind).ap()
    fK = nc.dram_tensor("fK", [D, TOKF], BF16, kind=skind).ap()
    fV = nc.dram_tensor("fV", [TOKF, D], BF16, kind=skind).ap()
    wfscr = nc.dram_tensor("wfscr", [NE, TOKO], F32, kind=skind).ap()

    dbg = {}
    if debug:
        for nm, shape, dt in [("d_xn", (128, DT * TOKF), BF16),
                              ("d_xr", (128, DT * TOKF), BF16),
                              ("d_x1", (128, DT * TOKF), BF16),
                              ("d_q", (128, DT * TOKO), BF16),
                              ("d_g", (128, DT * TOKO), BF16),
                              ("d_fq", (128, DT * TOKO), BF16),
                              ("d_xno", (128, DT * TOKO), BF16),
                              ("d_st", (128, DT * TOKO), BF16),
                              ("d_og", (128, DT * TOKO), BF16),
                              ("d_s", (128, DT * TOKO), BF16),
                              ("d_attn", (128, DT * TOKO), BF16),
                              ("d_hid", (128, DT * TOKO), F32),
                              ("d_x2", (128, DT * TOKO), BF16),
                              ("d_wf", (NE, TOKO), F32),
                              ("d_act0", (128, NIT * TOKO), BF16),
                              ("d_ffo", (128, DT * TOKO), F32)]:
            dbg[nm] = nc.dram_tensor(nm, list(shape), dt, kind="ExternalOutput").ap()

    stack = ExitStack()
    with stack:
        K = KCtx(nc, stack)
        probe_t = nc.alloc_sbuf_tensor("probe_base", [128, 16], F32)
        BASE = (nc.lookup_mloc(probe_t).addr + 64 + 63) // 64 * 64
        LIMIT = 229344
        C0 = BASE
        R0 = C0 + 6144
        A1 = R0 + 12288
        A2 = A1 + 65536
        A3 = A2 + 65536
        A4 = A3 + 32768
        assert A4 + 27648 <= LIMIT, (BASE, A4, LIMIT)

        at = K.at
        c_lam0 = at("c_lam0", [128, 512], BF16, C0 + 0)
        c_lam1 = at("c_lam1", [128, 512], BF16, C0 + 1024)
        c_ffw512 = at("c_ffw512", [128, 512], BF16, C0 + 2048)
        c_ffw256 = at("c_ffw256", [128, 256], BF16, C0 + 3072)
        c_id = at("c_id", [128, 128], BF16, C0 + 3584)
        c_idf = at("c_idf", [128, 128], F32, C0 + 3840)
        c_ocf = at("c_ocf", [128, 1], F32, C0 + 4352)
        c_ocb = at("c_ocb", [128, 1], BF16, C0 + 5888)
        c_orb = at("c_orb", [1, 128], BF16, C0 + 4416)
        c_orf = at("c_orf", [1, 128], F32, C0 + 4672)
        c_ln1 = at("c_ln1", [128, DT], F32, C0 + 5184)
        c_ln2 = at("c_ln2", [128, DT], F32, C0 + 5248)
        c_gwb = at("c_gwb", [128, DT * NE], BF16, C0 + 5312)
        c_eps = at("c_eps", [128, 4], F32, C0 + 5824)
        nc.const_aps.register(F32, EPS_LN, c_eps[:, 0:1])
        nc.const_aps.register(F32, EPS_RMS2, c_eps[:, 1:2])
        nc.const_aps.register(F32, EPS_LN1, c_eps[:, 2:3])
        nc.const_aps.register(F32, EPS_LN2, c_eps[:, 3:4])

        r_mean = at("r_mean", [1, 512], F32, R0 + 0)
        r_msq = at("r_msq", [1, 512], F32, R0 + 2048)
        r_var = at("r_var", [1, 512], F32, R0 + 4096)
        r_rstd = at("r_rstd", [1, 512], F32, R0 + 6144)
        r2o = at("r2o", [1, TOKO], F32, A4 + 16384)
        r1o = at("r1o", [1, TOKO], F32, A4 + 20480)
        r_mrows = [at("r_mrow0", [128, 1], F32, R0 + 0),
                   at("r_mrow1", [128, 1], F32, R0 + 512)]
        r_brows = [at("r_brow0", [128, 1], F32, R0 + 64),
                   at("r_brow1", [128, 1], F32, R0 + 576)]
        r_dens = [at("r_den0", [128, 1], F32, R0 + 128),
                  at("r_den1", [128, 1], F32, R0 + 640)]
        r_rdens = [at("r_rden0", [128, 1], F32, R0 + 192),
                   at("r_rden1", [128, 1], F32, R0 + 704)]
        r_L = at("r_L", [128, NE], F32, R0 + 1024)
        r_Lw = at("r_Lw", [128, NE], F32, R0 + 1088)
        r_ge = at("r_ge", [128, NE], F32, R0 + 1152)
        r_m1 = at("r_m1", [128, 1], F32, R0 + 1216)
        r_th = at("r_th", [128, 1], F32, R0 + 1280)
        r_es = at("r_es", [128, 1], F32, R0 + 1344)
        r_wsum = at("r_wsum", [128, 1], F32, R0 + 1408)
        r_wft = at("r_wft", [128, NE], F32, R0 + 1472)
        r_tsil = at("r_tsil", [128, 512], BF16, R0 + 2048)
        r_tsil2 = at("r_tsil2", [128, 512], BF16, R0 + 3072)
        r_xtmp = at("r_xtmp", [128, 512], F32, R0 + 4096)
        r_wrow = at("r_wrow", [1, TOKO], BF16, R0 + 8192)
        r_wb = at("r_wb", [128, TOKO], BF16, R0 + 10240)

        xn_full = at("xn_full", [128, DT * TOKF], BF16, A1)
        xbufA = at("xbufA", [128, DT * 256], F32, A2)
        xbufB = at("xbufB", [128, DT * 256], F32, A2 + 16384)
        sq_sp = at("sq_sp", [128, DT * 256], F32, A2 + 32768)
        xr_full = at("xr_full", [128, DT * TOKF], BF16, A2)
        ebufA = at("ebufA", [128, 512], BF16, A3 + 0)
        ebufB = at("ebufB", [128, 512], BF16, A3 + 1024)
        r2_all = at("r2_all", [1, TOKF], F32, A3 + 16384)
        r1_all = at("r1_all", [1, TOKF], F32, A3 + 24576)

        xn_own = at("xn_own", [128, DT * TOKO], BF16, A1)
        q_sb = at("q_sb", [128, DT * TOKO], BF16, A1 + 32768)
        g_sb = at("g_sb", [128, DT * TOKO], BF16, A2)
        fq_sb = at("fq_sb", [128, DT * TOKO], BF16, A2 + 32768)
        xobufA = at("xobufA", [128, DT * 256], F32, A3)
        xobufB = at("xobufB", [128, DT * 256], F32, A3 + 16384)
        sq_o = at("sq_o", [128, DT * 256], F32, A4 + 0)
        xro = at("xro", [128, DT * TOKO], BF16, A3)
        wbA = at("wbA", [128, DT * 128], BF16, A4 + 0)
        wbB = at("wbB", [128, DT * 128], BF16, A4 + 4096)
        wvA = at("wvA", [128, DT * 256], BF16, A4 + 8192)
        wvB = at("wvB", [128, DT * 256], BF16, A4 + 16384)

        st_sb = at("st_sb", [128, DT * TOKO], BF16, A3)
        og_sb = at("og_sb", [128, DT * TOKO], BF16, A1 + 32768)
        s_sb = at("s_sb", [128, DT * TOKO], BF16, A3)
        kjA = at("kjA", [128, DT * 128], BF16, A4 + 0)
        kjB = at("kjB", [128, DT * 128], BF16, A4 + 4096)
        vlA = at("vlA", [128, 512], BF16, A4 + 8192)
        vlB = at("vlB", [128, 512], BF16, A4 + 9216)
        wpA = at("wpA", [128, DT * 128], BF16, A4 + 10240)
        wpB = at("wpB", [128, DT * 128], BF16, A4 + 14336)

        vbandA = at("vbandA", [128, DT * 256], BF16, A4 + 0)
        vbandB = at("vbandB", [128, DT * 256], BF16, A4 + 8192)
        kbs = [at(f"kb{i}", [128, TOKF], BF16, A4 + i * 4096) for i in range(3)]
        vbs = [at(f"vb{i}", [128, TOKF], BF16, A4 + 12288 + i * 4096)
               for i in range(3)]
        owA = at("owA", [128, DT * 128], BF16, A4 + 0)
        owB = at("owB", [128, DT * 128], BF16, A4 + 4096)
        S_sbs = [at("S_sb0", [128, TOKF], F32, A2),
                 at("S_sb1", [128, TOKF], F32, A2 + 16384)]
        P_sbs = [at("P_sb0", [128, TOKF], BF16, A2 + 8192),
                 at("P_sb1", [128, TOKF], BF16, A2 + 24576)]
        Pt_sbs = [at("Pt_sb0", [128, TOKF], BF16, A2 + 12288),
                  at("Pt_sb1", [128, TOKF], BF16, A2 + 28672)]
        attn_sb = at("attn_sb", [128, DT * TOKO], BF16, A1 + 32768)
        hid_sb = at("hid_sb", [128, DT * TOKO], F32, A2)
        x2_sb = at("x2_sb", [128, DT * TOKO], BF16, A1)
        p5sqA = at("p5sqA", [128, 512], F32, A4 + 0)
        p5sqB = at("p5sqB", [128, 512], F32, A4 + 2048)
        r_wffm = at("r_wffm", [16, TOKO], F32, A4 + 4096)

        acts_sb = at("acts_sb", [128, NIT * TOKO], BF16, A1 + 32768)
        w1A = at("w1A", [128, 2 * DT * 128], BF16, A4 + 0)
        w1B = at("w1B", [128, 2 * DT * 128], BF16, A4 + 8192)
        w2A = at("w2A", [128, NIT * 256], BF16, A4 + 16384)
        w2B = at("w2B", [128, NIT * 256], BF16, A4 + 22016)

        g2_sb = at("g2_sb", [128, DT * TOKO], BF16, A1)
        xo7A = at("xo7A", [128, DT * 128], F32, A4 + 0)
        xo7B = at("xo7B", [128, DT * 128], F32, A4 + 8192)
        out7s = [at("out7a", [128, DT * 128], F32, A4 + 16384),
                 at("out7b", [128, DT * 128], F32, A1 + 32768)]

        P = Plan(K)
        for dst, src in [(c_lam0, la_m0w), (c_lam1, la_m1w),
                         (c_ffw512, ff_w512), (c_ffw256, ff_w256),
                         (c_id, identb), (c_idf, identf),
                         (c_ocf, ones_colf), (c_ocb, ones_colb),
                         (c_orb, ones_rowb),
                         (c_orf, ones_rowf), (c_ln1, ln1_r), (c_ln2, ln2_r),
                         (c_eps, epsv)]:
            P.dma("sync", dst[:], src, "dio")
        P.dma("gpsimd", c_gwb[:],
              ap2(gw_t, 0, [[NE, 128], [128 * NE, DT], [1, NE]]), "dwA")
        P.run(nc, "p0")
        K.const_waits = [("dio", K.cnt["dio"]), ("dwA", K.cnt["dwA"])]

        if upto == "p0":

            return nc, dbg


        def emit_ln_phase(P, src_dram, n_tok, xn_dst, r2_dst, r1_dst,
                          bufs, sqb, psS, psQ, psBr, psBm):
            W = 256
            nsp = n_tok // W
            prev = {"v_xn": {}, "pe_s": {}, "pe_q2": {}, "v_rows": {}}
            lds = {}
            spandone = {}

            def issue_load(f):
                if f >= nsp:
                    return
                xb = bufs[f % 2]
                lsem = "kA" if f % 2 == 0 else "kB"
                lw = []
                if f >= 2:
                    lw = [("dve", prev["v_xn"][f - 2]), ("pe", prev["pe_s"][f - 2])]
                lds[f] = P.dma("sync", xb[:],
                               ap2(src_dram, f * W,
                                   [[n_tok, 128], [128 * n_tok, DT], [1, W]]),
                               lsem, waits=lw)

            issue_load(0)
            issue_load(1)
            for f in range(nsp):
                xb = bufs[f % 2]
                lsem = "kA" if f % 2 == 0 else "kB"
                ld = lds[f]
                for dt in range(DT):
                    P.op("vector", lambda e, dt=dt, xb=xb: e.tensor_mul(
                        sqb[:, dt * W:(dt + 1) * W],
                        xb[:, dt * W:(dt + 1) * W], xb[:, dt * W:(dt + 1) * W]),
                        waits=[(lsem, ld), ("pe", prev["pe_q2"].get(f - 1, 0))],
                        inc="dve" if dt == DT - 1 else None)
                v_sq = K.cnt["dve"]
                for dt in range(DT):
                    P.op("tensor", lambda e, dt=dt, xb=xb: e.matmul(
                        psS[:, :], c_ocf[:], xb[:, dt * W:(dt + 1) * W],
                        start=(dt == 0), stop=(dt == DT - 1)),
                        waits=[(lsem, ld), ("dve", prev["v_rows"].get(f - 1, 0))],
                        inc="pe" if dt == DT - 1 else None)
                pe_s = K.cnt["pe"]
                prev["pe_s"][f] = pe_s
                for dt in range(DT):
                    P.op("tensor", lambda e, dt=dt: e.matmul(
                        psQ[:, :], c_ocf[:], sqb[:, dt * W:(dt + 1) * W],
                        start=(dt == 0), stop=(dt == DT - 1)),
                        waits=[("dve", v_sq)],
                        inc="pe" if dt == DT - 1 else None)
                pe_q = K.cnt["pe"]
                v = P.op("vector", lambda e: e.tensor_scalar_mul(
                    r_mean[:, 0:W], psS[0:1, :], 1.0 / D),
                    waits=[("pe", pe_s)], inc="dve")
                v = P.op("vector", lambda e: e.tensor_scalar_mul(
                    r_msq[:, 0:W], psQ[0:1, :], 1.0 / D),
                    waits=[("pe", pe_q)], inc="dve")
                v = P.op("vector", lambda e: e.tensor_mul(
                    r_var[:, 0:W], r_mean[:, 0:W], r_mean[:, 0:W]),
                    waits=[("dve", v)], inc="dve")
                v = P.op("vector", lambda e: e.tensor_sub(
                    r_var[:, 0:W], r_msq[:, 0:W], r_var[:, 0:W]),
                    waits=[("dve", v)], inc="dve")
                a = P.op("scalar", lambda e: e.activation(
                    r_msq[:, 0:W], r_var[:, 0:W], AF.Ln, bias=EPS_LN),
                    waits=[("dve", v)], inc="act")
                a = P.op("scalar", lambda e: e.activation(
                    r_rstd[:, 0:W], r_msq[:, 0:W], AF.Exp, scale=-0.5),
                    waits=[("act", a)], inc="act")
                v = P.op("vector", lambda e: e.tensor_mul(
                    r_mean[:, 0:W], r_mean[:, 0:W], r_rstd[:, 0:W]),
                    waits=[("act", a)], inc="dve")
                v_mr = v
                pbr = P.op("tensor", lambda e: e.matmul(
                    psBr[:, :], c_orf[:], r_rstd[0:1, 0:W], start=True, stop=True),
                    waits=[("act", a), ("dve", prev["v_xn"].get(f - 1, 0))], inc="pe")
                pbm = P.op("tensor", lambda e: e.matmul(
                    psBm[:, :], c_orf[:], r_mean[0:1, 0:W], start=True, stop=True),
                    waits=[("dve", v_mr)], inc="pe")
                for dt in range(DT):
                    P.op("vector", lambda e, dt=dt, xb=xb: e.tensor_mul(
                        xb[:, dt * W:(dt + 1) * W],
                        xb[:, dt * W:(dt + 1) * W], psBr[:, :]),
                        waits=[("pe", pbr)])
                for dt in range(DT):
                    P.op("vector", lambda e, dt=dt, f=f, xb=xb: e.tensor_sub(
                        xn_dst[:, dt * n_tok + f * W: dt * n_tok + f * W + W],
                        xb[:, dt * W:(dt + 1) * W], psBm[:, :]),
                        waits=[("pe", pbm)],
                        inc="dve" if dt == DT - 1 else None)
                v_xn = K.cnt["dve"]
                prev["v_xn"][f] = v_xn
                v = P.op("vector", lambda e: e.tensor_mul(
                    r_msq[:, 0:W], r_var[:, 0:W], r_rstd[:, 0:W]),
                    waits=[("dve", v_mr)], inc="dve")
                v = P.op("vector", lambda e: e.tensor_mul(
                    r_msq[:, 0:W], r_msq[:, 0:W], r_rstd[:, 0:W]),
                    waits=[("dve", v)], inc="dve")
                a = P.op("scalar", lambda e: e.activation(
                    r_var[:, 0:W], r_msq[:, 0:W], AF.Ln, bias=EPS_RMS2),
                    waits=[("dve", v)], inc="act")
                a = P.op("scalar", lambda e, f=f: e.activation(
                    r2_dst[:, f * W:(f + 1) * W], r_var[:, 0:W], AF.Exp, scale=-0.5),
                    waits=[("act", a)], inc="act")
                a = P.op("scalar", lambda e: e.activation(
                    r_var[:, 0:W], r_msq[:, 0:W], AF.Ln, bias=EPS_LN1),
                    waits=[("act", a)], inc="act")
                a = P.op("scalar", lambda e, f=f: e.activation(
                    r1_dst[:, f * W:(f + 1) * W], r_var[:, 0:W], AF.Exp, scale=-0.5),
                    waits=[("act", a)], inc="act")
                v = P.op("vector", lambda e: e.tensor_copy(
                    r_mean[0:1, 0:1], r_mean[0:1, 0:1]),
                    waits=[("act", a)], inc="dve")
                prev["v_rows"][f] = v
                prev["pe_q2"][f] = pe_q
                spandone[f] = (K.cnt["act"], prev["v_xn"][f])
                issue_load(f + 2)
            return spandone

        def emit_derive_full(P, psB, rows, lnw, dst, spandone=None):
            vprev = 0
            for f in range(4):
                dw = [("dve", vprev)]
                if spandone is not None:
                    act_c, dve_c = spandone[2 * f + 1]
                    dw += [("act", act_c), ("dve", dve_c)]
                pb = P.op("tensor", lambda e, f=f: e.matmul(
                    psB[:, :], c_orf[:], rows[:, f * 512:(f + 1) * 512],
                    start=True, stop=True),
                    waits=dw, inc="pe")
                for dt in range(DT):
                    if lnw is None:
                        P.op("vector", lambda e, dt=dt, f=f: e.tensor_mul(
                            dst[:, dt * TOKF + f * 512: dt * TOKF + f * 512 + 512],
                            xn_full[:, dt * TOKF + f * 512: dt * TOKF + f * 512 + 512],
                            psB[:, :]), waits=[("pe", pb)],
                            inc="dve" if dt == DT - 1 else None)
                    else:
                        P.op("vector", lambda e, dt=dt, f=f: e.tensor_mul(
                            r_xtmp[:, :],
                            xn_full[:, dt * TOKF + f * 512: dt * TOKF + f * 512 + 512],
                            psB[:, :]), waits=[("pe", pb)])
                        P.op("vector", lambda e, dt=dt, f=f: e.tensor_mul(
                            dst[:, dt * TOKF + f * 512: dt * TOKF + f * 512 + 512],
                            r_xtmp[:, :],
                            lnw[:, dt:dt + 1].to_broadcast([128, 512])),
                            inc="dve" if dt == DT - 1 else None)
                vprev = K.cnt["dve"]
            return vprev

        with ExitStack() as pc:
            psS = pc.enter_context(nc.psum_tensor("psS", [1, 256], F32))
            psQ = pc.enter_context(nc.psum_tensor("psQ", [1, 256], F32))
            psBr = pc.enter_context(nc.psum_tensor("psBr", [128, 256], F32))
            psBm = pc.enter_context(nc.psum_tensor("psBm", [128, 256], F32))
            psB2a = pc.enter_context(nc.psum_tensor("psB2a", [128, 512], F32))
            P = Plan(K)
            sd = emit_ln_phase(P, xfT, TOKF, xn_full, r2_all, r1_all,
                               (xbufA, xbufB), sq_sp, psS, psQ, psBr, psBm)
            emit_derive_full(P, psB2a, r2_all, None, xr_full, spandone=sd)
            if debug:
                P.dma("sync", dbg["d_xn"], xn_full[:], "dio",
                      waits=[("dve", K.cnt["dve"]), ("act", K.cnt["act"])])
                P.dma("sync", dbg["d_xr"], xr_full[:], "dio",
                      waits=[("dve", K.cnt["dve"])])
            P.run(nc, "p1")
            if upto == "p1":
                return nc, dbg


        def stream_fm_to_scratch(P, psA, psB, w_dram, w_row_len, w_col0, n_out,
                                 act, dst_dram, dst_row0, src=None):
            evac = {}
            for c in range(n_out // 128):
                wb = wbA if c % 2 == 0 else wbB
                wsem = "dwA" if c % 2 == 0 else "dwB"
                wwaits = [("pe", evac[("pef", c - 2)])] if c >= 2 else []
                wld = P.dma("gpsimd", wb[:],
                            ap2(w_dram, w_col0 + c * 128,
                                [[w_row_len, 128], [128 * w_row_len, DT], [1, 128]]),
                            wsem, waits=wwaits)
                for f in range(4):
                    idx = c * 4 + f
                    ps = psA if idx % 2 == 0 else psB
                    mmw = [(wsem, wld)]
                    if ("ev", idx - 2) in evac:
                        mmw.append(evac[("ev", idx - 2)])
                    srct = xr_full if src is None else src
                    for dt in range(DT):
                        P.op("tensor", lambda e, dt=dt, f=f, ps=ps, wb=wb, srct=srct: e.matmul(
                            ps[:, :], wb[:, dt * 128:(dt + 1) * 128],
                            srct[:, dt * TOKF + f * 512: dt * TOKF + f * 512 + 512],
                            start=(dt == 0), stop=(dt == DT - 1)),
                            waits=mmw if dt == 0 else [],
                            inc="pe" if dt == DT - 1 else None)
                    pe_c = K.cnt["pe"]
                    evac[("pef", c)] = pe_c
                    eb = ebufA if idx % 2 == 0 else ebufB
                    ssem = "sA" if idx % 2 == 0 else "sB"
                    a = P.op("scalar", lambda e, ps=ps, eb=eb: e.activation(
                        eb[:, :], ps[:, :], act),
                        waits=[("pe", pe_c), (ssem, K.cnt[ssem])], inc="act")
                    evac[("ev", idx)] = ("act", a)
                    P.dma("sync",
                          ap2(dst_dram, (dst_row0 + c * 128) * TOKF + f * 512,
                              [[TOKF, 128], [1, 512]]),
                          eb[:, :], ssem, waits=[("act", a)])

        def stream_tm_to_scratch(P, psA, psB, w_dram, w_row_len, w_col0, n_v,
                                 act, dst_dram, src=None):
            evac = {}
            for vc in range(n_v // 256):
                wb = wvA if vc % 2 == 0 else wvB
                wsem = "dwA" if vc % 2 == 0 else "dwB"
                wwaits = [("pe", evac[("pef", vc - 2)])] if vc >= 2 else []
                wld = P.dma("gpsimd", wb[:],
                            ap2(w_dram, w_col0 + vc * 256,
                                [[w_row_len, 128], [128 * w_row_len, DT], [1, 256]]),
                            wsem, waits=wwaits)
                for tt in range(DT):
                    idx = vc * DT + tt
                    ps = psA if idx % 2 == 0 else psB
                    mmw = [(wsem, wld)]
                    if ("ev", idx - 2) in evac:
                        mmw.append(evac[("ev", idx - 2)])
                    srct = xr_full if src is None else src
                    for dt in range(DT):
                        P.op("tensor", lambda e, dt=dt, tt=tt, ps=ps, wb=wb, srct=srct: e.matmul(
                            ps[:, 0:256],
                            srct[:, dt * TOKF + tt * 128: dt * TOKF + tt * 128 + 128],
                            wb[:, dt * 256:(dt + 1) * 256],
                            start=(dt == 0), stop=(dt == DT - 1)),
                            waits=mmw if dt == 0 else [],
                            inc="pe" if dt == DT - 1 else None)
                    pe_c = K.cnt["pe"]
                    evac[("pef", vc)] = pe_c
                    eb = ebufA if idx % 2 == 0 else ebufB
                    ssem = "sA" if idx % 2 == 0 else "sB"
                    a = P.op("scalar", lambda e, ps=ps, eb=eb: e.activation(
                        eb[:, 0:256], ps[:, 0:256], act),
                        waits=[("pe", pe_c), (ssem, K.cnt[ssem])], inc="act")
                    evac[("ev", idx)] = ("act", a)
                    P.dma("sync",
                          ap2(dst_dram, (tt * 128) * D + vc * 256,
                              [[D, 128], [1, 256]]),
                          eb[:, 0:256], ssem, waits=[("act", a)])

        with ExitStack() as pc:
            psA = pc.enter_context(nc.psum_tensor("psA2", [128, 512], F32))
            psB = pc.enter_context(nc.psum_tensor("psB2b", [128, 512], F32))
            P = Plan(K)
            stream_fm_to_scratch(P, psA, psB, wk_t, D, 0, D, AF.Silu, laK, 0)
            P.run(nc, "p2b")
            if upto == "p2b":
                return nc, dbg
            psB2d = pc.enter_context(nc.psum_tensor("psB2d", [128, 512], F32))
            P = Plan(K)
            stream_tm_to_scratch(P, psA, psB, wv_t, D, 0, D, AF.Silu, laV)
            emit_derive_full(P, psB2d, r1_all, c_ln1, xn_full)
            if debug:
                P.dma("sync", dbg["d_x1"], xn_full[:], "dio",
                      waits=[("dve", K.cnt["dve"])])
            P.run(nc, "p2c")
        if upto == "p2c":
            return nc, dbg
        with ExitStack() as pc:
            psA = pc.enter_context(nc.psum_tensor("psA2e", [128, 512], F32))
            psB = pc.enter_context(nc.psum_tensor("psB2e", [128, 512], F32))
            P = Plan(K)
            stream_fm_to_scratch(P, psA, psB, kvw_t, 2 * D, 0, D, AF.Copy, fK, 0,
                                 src=xn_full)
            P.run(nc, "p2e")
            if upto == "p2e":
                return nc, dbg
            P = Plan(K)
            stream_tm_to_scratch(P, psA, psB, kvw_t, 2 * D, D, D, AF.Copy, fV,
                                 src=xn_full)
            P.run(nc, "p2f")

        if upto == "p2f":

            return nc, dbg

        with ExitStack() as pc:
            psS = pc.enter_context(nc.psum_tensor("psS3", [1, 256], F32))
            psQ = pc.enter_context(nc.psum_tensor("psQ3", [1, 256], F32))
            psBr = pc.enter_context(nc.psum_tensor("psBr3", [128, 256], F32))
            psBm = pc.enter_context(nc.psum_tensor("psBm3", [128, 256], F32))
            P = Plan(K)
            emit_ln_phase(P, xoT, TOKO, xn_own, r2o, r1o,
                          (xobufA, xobufB), sq_o, psS, psQ, psBr, psBm)
            if debug:
                P.dma("sync", dbg["d_xno"], xn_own[:], "dio",
                      waits=[("dve", K.cnt["dve"])])
            P.run(nc, "p3a")
            if upto == "p3a":
                return nc, dbg

        def emit_derive_own(P, psB, rows, lnw):
            vprev = 0
            for f in range(2):
                pb = P.op("tensor", lambda e, f=f: e.matmul(
                    psB[:, :], c_orf[:], rows[:, f * 512:(f + 1) * 512],
                    start=True, stop=True), waits=[("dve", vprev)], inc="pe")
                for dt in range(DT):
                    if lnw is None:
                        P.op("vector", lambda e, dt=dt, f=f: e.tensor_mul(
                            xro[:, dt * TOKO + f * 512: dt * TOKO + f * 512 + 512],
                            xn_own[:, dt * TOKO + f * 512: dt * TOKO + f * 512 + 512],
                            psB[:, :]), waits=[("pe", pb)],
                            inc="dve" if dt == DT - 1 else None)
                    else:
                        P.op("vector", lambda e, dt=dt, f=f: e.tensor_mul(
                            r_xtmp[:, :],
                            xn_own[:, dt * TOKO + f * 512: dt * TOKO + f * 512 + 512],
                            psB[:, :]), waits=[("pe", pb)])
                        P.op("vector", lambda e, dt=dt, f=f: e.tensor_mul(
                            xro[:, dt * TOKO + f * 512: dt * TOKO + f * 512 + 512],
                            r_xtmp[:, :],
                            lnw[:, dt:dt + 1].to_broadcast([128, 512])),
                            inc="dve" if dt == DT - 1 else None)
                vprev = K.cnt["dve"]
            return vprev

        def stream_proj_own(P, psA, psB, w_dram, act, dst_sb, src_rdy):
            evac = {}
            pe0 = K.cnt["pe"]
            for c in range(DT):
                wb = wbA if c % 2 == 0 else wbB
                wsem = "dwA" if c % 2 == 0 else "dwB"
                wwaits = [("pe", evac[("pef", c - 2)])] if c >= 2 else [("pe", pe0)]
                wld = P.dma("gpsimd", wb[:],
                            ap2(w_dram, c * 128,
                                [[D, 128], [128 * D, DT], [1, 128]]),
                            wsem, waits=wwaits)
                for f in range(2):
                    idx = c * 2 + f
                    ps = psA if idx % 2 == 0 else psB
                    mmw = [(wsem, wld), ("dve", src_rdy)]
                    if ("ev", idx - 2) in evac:
                        mmw.append(evac[("ev", idx - 2)])
                    for dt in range(DT):
                        P.op("tensor", lambda e, dt=dt, f=f, ps=ps, wb=wb: e.matmul(
                            ps[:, :], wb[:, dt * 128:(dt + 1) * 128],
                            xro[:, dt * TOKO + f * 512: dt * TOKO + f * 512 + 512],
                            start=(dt == 0), stop=(dt == DT - 1)),
                            waits=mmw if dt == 0 else [],
                            inc="pe" if dt == DT - 1 else None)
                    pe_c = K.cnt["pe"]
                    evac[("pef", c)] = pe_c
                    a = P.op("scalar", lambda e, ps=ps, c=c, f=f, dst_sb=dst_sb: e.activation(
                        dst_sb[:, c * TOKO + f * 512: c * TOKO + f * 512 + 512],
                        ps[:, :], act),
                        waits=[("pe", pe_c)], inc="act")
                    evac[("ev", idx)] = ("act", a)

        with ExitStack() as pc:
            psA = pc.enter_context(nc.psum_tensor("psA3", [128, 512], F32))
            psB = pc.enter_context(nc.psum_tensor("psB3", [128, 512], F32))
            psBc = pc.enter_context(nc.psum_tensor("psBc3", [128, 512], F32))
            P = Plan(K)
            vr = emit_derive_own(P, psBc, r2o, None)
            stream_proj_own(P, psA, psB, wq_t, AF.Silu, q_sb, vr)
            stream_proj_own(P, psA, psB, wg_t, AF.Sigmoid, g_sb, vr)
            P.run(nc, "p3b")
            if upto == "p3b":
                return nc, dbg
            P = Plan(K)
            vr = emit_derive_own(P, psBc, r1o, c_ln1)
            stream_proj_own(P, psA, psB, qw_t, AF.Copy, fq_sb, vr)
            if debug:
                ac = K.cnt["act"]
                P.dma("sync", dbg["d_q"], q_sb[:], "dio", waits=[("act", ac)])
                P.dma("sync", dbg["d_g"], g_sb[:], "dio", waits=[("act", ac)])
                P.dma("sync", dbg["d_fq"], fq_sb[:], "dio", waits=[("act", ac)])
            P.run(nc, "p3c")

        if upto == "p3c":

            return nc, dbg

        with ExitStack() as pc:
            stA = pc.enter_context(nc.psum_tensor("stA", [128, 512], F32))
            stB = pc.enter_context(nc.psum_tensor("stB", [128, 512], F32))
            P = Plan(K)
            evac = {}
            ci = 0
            P.op("vector", lambda e: e.memset(st_sb[:], 0), inc="dve")
            for j in range(min(DT, P4A_JMAX)):
                kb = kjA if j % 2 == 0 else kjB
                ksem = "kA" if j % 2 == 0 else "kB"
                kwaits = [("pe", evac[("kf", j - 2)])] if j >= 2 else [
                    ("sA", K.cnt["sA"]), ("sB", K.cnt["sB"])]
                kld = P.dma("sync", kb[:],
                            ap2(laK, j * 128,
                                [[TOKF, 128], [128 * TOKF, DT], [1, 128]]),
                            ksem, waits=kwaits)
                smin = j // 2
                c0 = smin * 128
                chunks = [(c0, 512), (512, 1024)] if c0 < 512 else [(c0, 1024)]
                for (a0, a1) in chunks:
                    ps = stA if ci % 2 == 0 else stB
                    mmw = [(ksem, kld)]
                    if ("ev", ci - 2) in evac:
                        mmw.extend(evac[("ev", ci - 2)])
                    for dt in range(DT):
                        P.op("tensor", lambda e, dt=dt, a0=a0, a1=a1, ps=ps, kb=kb: e.matmul(
                            ps[:, 0:a1 - a0], kb[:, dt * 128:(dt + 1) * 128],
                            q_sb[:, dt * TOKO + a0: dt * TOKO + a1],
                            start=(dt == 0), stop=(dt == DT - 1)),
                            waits=mmw if dt == 0 else [],
                            inc="pe" if dt == DT - 1 else None)
                    pe_c = K.cnt["pe"]
                    evac[("kf", j)] = pe_c
                    if a0 == c0:
                        msk = c_lam0 if j % 2 == 0 else c_lam1
                        w = a1 - a0
                        vv = P.op("vector", lambda e, ps=ps, j=j, a0=a0, w=w, msk=msk: e.tensor_mul(
                            st_sb[:, j * TOKO + a0: j * TOKO + a0 + w],
                            ps[:, 0:w], msk[:, 0:w]),
                            waits=[("pe", pe_c)], inc="dve")
                        evac[("ev", ci)] = [("dve", vv)]
                    else:
                        aa = P.op("scalar", lambda e, ps=ps, j=j, a0=a0, a1=a1: e.activation(
                            st_sb[:, j * TOKO + a0: j * TOKO + a1],
                            ps[:, 0:a1 - a0], AF.Copy),
                            waits=[("pe", pe_c)], inc="act")
                        evac[("ev", ci)] = [("act", aa)]
                    ci += 1
            if debug:
                P.dma("sync", dbg["d_st"], st_sb[:], "dio",
                      waits=[("dve", K.cnt["dve"]), ("act", K.cnt["act"])])
            P.run(nc, "p4a")

        if upto == "p4a":

            return nc, dbg

        with ExitStack() as pc:
            po = [pc.enter_context(nc.psum_tensor(f"po{v}", [128, 1024], F32))
                  for v in range(4)]
            vband = [vbandA, vbandB]
            P = Plan(K)
            evac_done = {}
            pe_after = {}
            lds = {}

            def issue_band(vg):
                if vg >= 8:
                    return
                vsem = "vA" if vg % 2 == 0 else "vB"
                vw = [("pe", pe_after[vg - 2])] if vg >= 2 else [
                    ("sA", K.cnt["sA"]), ("sB", K.cnt["sB"])]
                lds[vg] = P.dma("sync", vband[vg % 2][:, 0:DT * 256],
                                ap2(laV, vg * 256,
                                    [[D, 128], [128 * D, DT], [1, 256]]),
                                vsem, waits=vw)

            issue_band(0)
            issue_band(1)
            for vg in range(8):
                vb = vband[vg % 2]
                vsem = "vA" if vg % 2 == 0 else "vB"
                vld = lds[vg]
                pbase = 2 * (vg % 2)
                first = [[True, True] for _ in range(2)]
                for j in range(DT):
                    smin = j // 2
                    c0 = smin * 128
                    chunks = [(c0, 512), (512, 1024)] if c0 < 512 else [(c0, 1024)]
                    for v in range(2):
                        for (a0, a1) in chunks:
                            bank = 0 if a0 < 512 else 1
                            st_flag = first[v][bank]
                            first[v][bank] = False
                            stop_flag = (j == 7 and bank == 0) or (j == DT - 1 and bank == 1)
                            mmw = [(vsem, vld)]
                            if vg >= 2 and j == 0:
                                mmw.append(("dve", evac_done[vg - 2]))
                            last_mm = (v == 1 and (a0, a1) == chunks[-1] and j == DT - 1)
                            P.op("tensor", lambda e, v=v, j=j, a0=a0, a1=a1, vb=vb,
                                 pbase=pbase, st_flag=st_flag, stop_flag=stop_flag: e.matmul(
                                po[pbase + v][:, a0:a1],
                                vb[:, j * 256 + v * 128: j * 256 + v * 128 + 128],
                                st_sb[:, j * TOKO + a0: j * TOKO + a1],
                                start=st_flag, stop=stop_flag),
                                waits=mmw,
                                inc="pe" if last_mm else None)
                pe_c = K.cnt["pe"]
                pe_after[vg] = pe_c
                issue_band(vg + 2)
                for v in range(2):
                    tile = vg * 2 + v
                    P.op("vector", lambda e, v=v, tile=tile, pbase=pbase: e.tensor_mul(
                        og_sb[:, tile * TOKO:(tile + 1) * TOKO],
                        po[pbase + v][:, :], g_sb[:, tile * TOKO:(tile + 1) * TOKO]),
                        waits=[("pe", pe_c)],
                        inc="dve" if v == 1 else None)
                evac_done[vg] = K.cnt["dve"]
            if debug:
                P.dma("sync", dbg["d_og"], og_sb[:], "dio",
                      waits=[("dve", K.cnt["dve"])])
            P.run(nc, "p4b")
            if upto == "p4b":
                return nc, dbg

        with ExitStack() as pc:
            psA = pc.enter_context(nc.psum_tensor("psA4", [128, 512], F32))
            psB = pc.enter_context(nc.psum_tensor("psB4", [128, 512], F32))
            P = Plan(K)
            evac = {}
            for c in range(DT):
                wb = wpA if c % 2 == 0 else wpB
                wsem = "dwA" if c % 2 == 0 else "dwB"
                wwaits = [("pe", evac[("pef", c - 2)])] if c >= 2 else []
                wld = P.dma("gpsimd", wb[:],
                            ap2(wp_t, c * 128,
                                [[D, 128], [128 * D, DT], [1, 128]]),
                            wsem, waits=wwaits)
                for f in range(2):
                    idx = c * 2 + f
                    ps = psA if idx % 2 == 0 else psB
                    mmw = [(wsem, wld)]
                    if ("ev", idx - 2) in evac:
                        mmw.append(evac[("ev", idx - 2)])
                    for dt in range(DT):
                        P.op("tensor", lambda e, dt=dt, f=f, ps=ps, wb=wb: e.matmul(
                            ps[:, :], wb[:, dt * 128:(dt + 1) * 128],
                            og_sb[:, dt * TOKO + f * 512: dt * TOKO + f * 512 + 512],
                            start=(dt == 0), stop=(dt == DT - 1)),
                            waits=mmw if dt == 0 else [],
                            inc="pe" if dt == DT - 1 else None)
                    pe_c = K.cnt["pe"]
                    evac[("pef", c)] = pe_c
                    a = P.op("scalar", lambda e, ps=ps, c=c, f=f: e.activation(
                        s_sb[:, c * TOKO + f * 512: c * TOKO + f * 512 + 512],
                        ps[:, :], AF.Copy),
                        waits=[("pe", pe_c)], inc="act")
                    evac[("ev", idx)] = ("act", a)
            if debug:
                P.dma("sync", dbg["d_s"], s_sb[:], "dio",
                      waits=[("act", K.cnt["act"])])
            P.run(nc, "p4c")

        if upto == "p4c":

            return nc, dbg

        with ExitStack() as pc:
            sps = [pc.enter_context(nc.psum_tensor(f"sps{i}", [128, 512], F32))
                   for i in range(4)]
            pt_pss = [pc.enter_context(nc.psum_tensor(f"pt_ps{i}", [128, 1024], BF16))
                      for i in range(2)]
            pv_pss = [pc.enter_context(nc.psum_tensor(f"pv_ps{i}", [128, 128], F32))
                      for i in range(2)]
            P = Plan(K)
            pairs = [(s, h) for s in range(NSLOT) for h in range(NH)]
            ksems = ["kA", "kB", "kC"]
            vsems = ["vA", "vB", "vC"]
            D5 = {"pvdone": {}, "pvev": {}, "exp": {}, "pscale": {}, "tev": {},
                  "gci": 0, "spslast": {}, "smm": {}}

            def p5_kload(pi):
                if pi >= len(pairs):
                    return
                s, h = pairs[pi]
                jl = 256 * (s + 1)
                lb = pi % 3
                kw = [("pe", D5["pvdone"].get(pi - 4, 0))]
                if pi == 0:
                    kw = [("sA", K.cnt["sA"]), ("sB", K.cnt["sB"])]
                D5[("kld", pi)] = P.dma(
                    "sync", kbs[lb][:, 0:jl],
                    ap2(fK, (h * 128) * TOKF, [[TOKF, 128], [1, jl]]),
                    ksems[lb], waits=kw)

            def p5_vload(pi):
                if pi >= len(pairs):
                    return
                s, h = pairs[pi]
                njt = 2 * s + 2
                lb = pi % 3
                kw = [("pe", D5["pvdone"].get(pi - 3, 0))]
                if pi == 0:
                    kw = [("sA", K.cnt["sA"]), ("sB", K.cnt["sB"])]
                D5[("vld", pi)] = P.dma(
                    "sync", vbs[lb][:, 0:njt * 128],
                    ap2(fV, h * 128, [[D, 128], [128 * D, njt], [1, 128]]),
                    vsems[lb], waits=kw)

            def p5_smm(pi):
                s, h = pairs[pi]
                jl = 256 * (s + 1)
                lb = pi % 3
                kb = kbs[lb]
                nch = (jl + 511) // 512
                chunks = []
                for c in range(nch):
                    a0, a1 = c * 512, min(jl, c * 512 + 512)
                    gci = D5["gci"]
                    D5["gci"] = gci + 1
                    ps = sps[gci % 4]
                    mmw = [(ksems[lb], D5[("kld", pi)])]
                    if (gci % 4) in D5["spslast"]:
                        mmw.append(D5["spslast"][gci % 4])
                    P.op("tensor", lambda e, s=s, h=h, a0=a0, a1=a1, ps=ps, kb=kb: e.matmul(
                        ps[:, 0:a1 - a0],
                        fq_sb[:, h * TOKO + s * 128: h * TOKO + s * 128 + 128],
                        kb[:, a0:a1], start=True, stop=True),
                        waits=mmw, inc="pe")
                    chunks.append((a0, a1, ps, gci, K.cnt["pe"]))
                D5[("chunks", pi)] = chunks
                D5["smm"][pi] = K.cnt["pe"]

            def hb1(pi):
                s, h = pairs[pi]
                par = pi % 2
                jl = 256 * (s + 1)
                S_sb = S_sbs[par]
                ev_list = []
                for (a0, a1, ps, gci, pe_c) in D5[("chunks", pi)]:
                    w = a1 - a0
                    if a1 == jl:
                        msk = c_ffw512 if w == 512 else c_ffw256
                        vv = P.op("vector", lambda e, ps=ps, a0=a0, a1=a1, w=w,
                                  msk=msk, S_sb=S_sb: e.tensor_add(
                            S_sb[:, a0:a1], ps[:, 0:w], msk[:, 0:w]),
                            waits=[("pe", pe_c),
                                   ("act", D5["exp"].get(pi - 2, 0))],
                            inc="dve")
                        ev = ("dve", vv)
                    else:
                        aa = P.op("scalar", lambda e, ps=ps, a0=a0, a1=a1, w=w,
                                  S_sb=S_sb: e.activation(
                            S_sb[:, a0:a1], ps[:, 0:w], AF.Copy),
                            waits=[("pe", pe_c),
                                   ("act", D5["exp"].get(pi - 2, 0))], inc="act")
                        ev = ("act", aa)
                    ev_list.append(ev)
                    D5["spslast"][gci % 4] = ev
                D5[("evl", pi)] = ev_list
                par = pi % 2
                mrow, brow = r_mrows[par], r_brows[par]
                v = P.op("vector", lambda e, jl=jl, S_sb=S_sb, mrow=mrow: e.reduce_max(
                    mrow[:], S_sb[:, 0:jl], axis=AX.X, negate=True),
                    waits=ev_list + [("act", D5["exp"].get(pi - 2, 0))],
                    inc="dve")
                v = P.op("vector", lambda e, mrow=mrow, brow=brow: e.tensor_scalar_mul(
                    brow[:], mrow[:], SCALE_ATTN),
                    waits=[("dve", v)], inc="dve")
                D5[("brow", pi)] = v

            def st3(pi):
                s, h = pairs[pi]
                par = pi % 2
                jl = 256 * (s + 1)
                njt = 2 * s + 2
                lb = pi % 3
                vsem = vsems[lb]
                vld = D5[("vld", pi)]
                vb = vbs[lb]
                S_sb = S_sbs[par]
                P_sb = P_sbs[par]
                Pt_sb = Pt_sbs[par]
                pt_ps = pt_pss[par]
                pv_ps = pv_pss[par]
                mrow, brow, den, rden = (r_mrows[par], r_brows[par],
                                         r_dens[par], r_rdens[par])
                v = D5[("brow", pi)]
                a = P.op("scalar", lambda e, jl=jl, S_sb=S_sb, P_sb=P_sb,
                         brow=brow, den=den: e.activation(
                    P_sb[:, 0:jl], S_sb[:, 0:jl], AF.Exp,
                    bias=brow[:], scale=SCALE_ATTN, accum_out=den[:]),
                    waits=[("dve", v), ("pe", D5["pvdone"].get(pi - 2, 0)),
                           ("dve", D5["pscale"].get(pi - 2, 0))],
                    inc="act")
                D5["exp"][pi] = a
                v = P.op("vector", lambda e, den=den, rden=rden: e.reciprocal(
                    rden[:], den[:]), waits=[("act", a)], inc="dve")
                v = P.op("vector", lambda e, jl=jl, P_sb=P_sb, rden=rden: e.tensor_mul(
                    P_sb[:, 0:jl], P_sb[:, 0:jl],
                    rden[:].to_broadcast([128, jl])),
                    waits=[("dve", v)], inc="dve")
                D5["pscale"][pi] = v

            def st4(pi):
                s, h = pairs[pi]
                par = pi % 2
                jl = 256 * (s + 1)
                njt = 2 * s + 2
                lb = pi % 3
                vsem = vsems[lb]
                vld = D5[("vld", pi)]
                vb = vbs[lb]
                P_sb = P_sbs[par]
                Pt_sb = Pt_sbs[par]
                pt_ps = pt_pss[par]
                pv_ps = pv_pss[par]
                pscale_rdy = D5["pscale"][pi]
                pv_first = True
                for b0 in range(0, njt, 8):
                    bn = min(8, njt - b0)
                    for k in range(bn):
                        jt = b0 + k
                        mmw = [("act", D5["exp"][pi]), ("dve", pscale_rdy)]
                        if ("tevp", pi, b0 - 8) in D5:
                            mmw.append(D5[("tevp", pi, b0 - 8)])
                        elif b0 == 0 and pi >= 2:
                            mmw.append(D5["tev"].get(pi - 2, ("act", 0)))
                        P.op("tensor", lambda e, jt=jt, k=k, P_sb=P_sb, pt_ps=pt_ps: e.transpose(
                            pt_ps[:, k * 128:(k + 1) * 128],
                            P_sb[:, jt * 128:(jt + 1) * 128], c_id[:]),
                            waits=mmw if k == 0 else [],
                            inc="pe" if k == bn - 1 else None)
                    pe_t = K.cnt["pe"]
                    aa = P.op("scalar", lambda e, b0=b0, bn=bn, Pt_sb=Pt_sb, pt_ps=pt_ps: e.activation(
                        Pt_sb[:, b0 * 128:(b0 + bn) * 128],
                        pt_ps[:, 0:bn * 128], AF.Copy),
                        waits=[("pe", pe_t)], inc="act")
                    D5[("tevp", pi, b0)] = ("act", aa)
                    D5["tev"][pi] = ("act", aa)
                    for k in range(bn):
                        jt = b0 + k
                        mmw = [("act", aa), (vsem, vld),
                               ("dve", D5["pvev"].get(pi - 2, 0))]
                        last = (b0 + bn >= njt and k == bn - 1)
                        P.op("tensor", lambda e, jt=jt, vb=vb, pf=pv_first,
                             last=last, pv_ps=pv_ps, Pt_sb=Pt_sb: e.matmul(
                            pv_ps[:, :], vb[:, jt * 128:(jt + 1) * 128],
                            Pt_sb[:, jt * 128:(jt + 1) * 128],
                            start=pf, stop=last),
                            waits=mmw if k == 0 else [],
                            inc="pe" if last else None)
                        pv_first = False
                pe_pv = K.cnt["pe"]
                D5["pvdone"][pi] = pe_pv
                vv = P.op("vector", lambda e, s=s, h=h, pv_ps=pv_ps: e.tensor_copy(
                    attn_sb[:, h * TOKO + s * 128: h * TOKO + s * 128 + 128],
                    pv_ps[:, :]), waits=[("pe", pe_pv)], inc="dve")
                D5["pvev"][pi] = vv

            NPAIR = len(pairs)
            p5_kload(0)
            p5_kload(1)
            p5_kload(2)
            p5_vload(0)
            p5_vload(1)
            p5_smm(0)
            p5_smm(1)
            hb1(0)
            for pi in range(NPAIR):
                if pi + 1 < NPAIR:
                    hb1(pi + 1)
                st3(pi)
                if pi >= 1:
                    st4(pi - 1)
                p5_kload(pi + 3)
                p5_vload(pi + 2)
                if pi + 2 < NPAIR:
                    p5_smm(pi + 2)
            st4(NPAIR - 1)
            if debug:
                P.dma("sync", dbg["d_attn"], attn_sb[:], "dio",
                      waits=[("dve", K.cnt["dve"])])
            P.run(nc, "p5a")
            if upto == "p5a":
                return nc, dbg

        with ExitStack() as pc:
            psA = pc.enter_context(nc.psum_tensor("psA5", [128, 512], F32))
            psB = pc.enter_context(nc.psum_tensor("psB5", [128, 512], F32))
            P = Plan(K)
            evac = {}
            for c in range(DT):
                wb = owA if c % 2 == 0 else owB
                wsem = "dwA" if c % 2 == 0 else "dwB"
                wwaits = [("pe", evac[("pef", c - 2)])] if c >= 2 else []
                wld = P.dma("gpsimd", wb[:],
                            ap2(ow_t, c * 128,
                                [[D, 128], [128 * D, DT], [1, 128]]),
                            wsem, waits=wwaits)
                for f in range(2):
                    idx = c * 2 + f
                    ps = psA if idx % 2 == 0 else psB
                    mmw = [(wsem, wld)]
                    if ("ev", idx - 2) in evac:
                        mmw.append(evac[("ev", idx - 2)])
                    for dt in range(DT):
                        P.op("tensor", lambda e, dt=dt, f=f, ps=ps, wb=wb: e.matmul(
                            ps[:, :], wb[:, dt * 128:(dt + 1) * 128],
                            attn_sb[:, dt * TOKO + f * 512: dt * TOKO + f * 512 + 512],
                            start=(dt == 0), stop=(dt == DT - 1)),
                            waits=mmw if dt == 0 else [],
                            inc="pe" if dt == DT - 1 else None)
                    pe_c = K.cnt["pe"]
                    evac[("pef", c)] = pe_c
                    vv = P.op("vector", lambda e, ps=ps, c=c, f=f: e.tensor_add(
                        hid_sb[:, c * TOKO + f * 512: c * TOKO + f * 512 + 512],
                        ps[:, :],
                        xn_own[:, c * TOKO + f * 512: c * TOKO + f * 512 + 512]),
                        waits=[("pe", pe_c)], inc="dve")
                    evac[("ev", idx)] = ("dve", vv)
            if debug:
                P.dma("sync", dbg["d_hid"], hid_sb[:], "dio",
                      waits=[("dve", K.cnt["dve"])])
            P.run(nc, "p5b")

        if upto == "p5b":

            return nc, dbg

        with ExitStack() as pc:
            psQ = pc.enter_context(nc.psum_tensor("psQ5", [1, 512], F32))
            psBr = pc.enter_context(nc.psum_tensor("psBr5", [128, 512], F32))
            psG = pc.enter_context(nc.psum_tensor("psG5", [128, NE], F32))
            psT = pc.enter_context(nc.psum_tensor("psT5", [16, 128], F32))
            P = Plan(K)
            for f in range(2):
                mm_done = {}
                for dt in range(DT):
                    sqb = p5sqA if dt % 2 == 0 else p5sqB
                    sw = []
                    if dt >= 2:
                        sw = [("pe", mm_done[dt - 2])]
                    P.op("vector", lambda e, dt=dt, f=f, sqb=sqb: e.tensor_mul(
                        sqb[:, :],
                        hid_sb[:, dt * TOKO + f * 512: dt * TOKO + f * 512 + 512],
                        hid_sb[:, dt * TOKO + f * 512: dt * TOKO + f * 512 + 512]),
                        waits=sw, inc="dve")
                    v_s = K.cnt["dve"]
                    P.op("tensor", lambda e, dt=dt, sqb=sqb: e.matmul(
                        psQ[:, :], c_ocf[:], sqb[:, :],
                        start=(dt == 0), stop=(dt == DT - 1)),
                        waits=[("dve", v_s)], inc="pe")
                    mm_done[dt] = K.cnt["pe"]
                pe_q = K.cnt["pe"]
                v = P.op("vector", lambda e: e.tensor_scalar_mul(
                    r_mean[:], psQ[0:1, :], 1.0 / D),
                    waits=[("pe", pe_q)], inc="dve")
                a = P.op("scalar", lambda e: e.activation(
                    r_var[:], r_mean[:], AF.Ln, bias=EPS_LN2),
                    waits=[("dve", v)], inc="act")
                a = P.op("scalar", lambda e: e.activation(
                    r_rstd[:], r_var[:], AF.Exp, scale=-0.5),
                    waits=[("act", a)], inc="act")
                pb = P.op("tensor", lambda e: e.matmul(
                    psBr[:, :], c_orf[:], r_rstd[0:1, :], start=True, stop=True),
                    waits=[("act", a)], inc="pe")
                for dt in range(DT):
                    sqb = p5sqA if dt % 2 == 0 else p5sqB
                    P.op("vector", lambda e, dt=dt, f=f, sqb=sqb: e.tensor_mul(
                        sqb[:, :],
                        hid_sb[:, dt * TOKO + f * 512: dt * TOKO + f * 512 + 512],
                        psBr[:, :]), waits=[("pe", pb)], inc="dve")
                    v2 = K.cnt["dve"]
                    P.op("vector", lambda e, dt=dt, f=f, sqb=sqb: e.tensor_mul(
                        x2_sb[:, dt * TOKO + f * 512: dt * TOKO + f * 512 + 512],
                        sqb[:, :],
                        c_ln2[:, dt:dt + 1].to_broadcast([128, 512])),
                        waits=[("dve", v2)],
                        inc="dve" if dt == DT - 1 else None)
            v_x2 = K.cnt["dve"]
            v_gate_prev = v_x2
            for it in range(NSLOT):
                for dt in range(DT):
                    mmw = [("dve", v_gate_prev)] if dt == 0 else []
                    P.op("tensor", lambda e, dt=dt, it=it: e.matmul(
                        psG[:, :],
                        x2_sb[:, dt * TOKO + it * 128: dt * TOKO + it * 128 + 128],
                        c_gwb[:, dt * NE:(dt + 1) * NE],
                        start=(dt == 0), stop=(dt == DT - 1)),
                        waits=mmw, inc="pe" if dt == DT - 1 else None)
                pe_g = K.cnt["pe"]
                v = P.op("vector", lambda e: e.tensor_copy(r_L[:], psG[:, :]),
                         waits=[("pe", pe_g)], inc="dve")
                v_gate_prev = v
                v = P.op("vector", lambda e: e.tensor_copy(r_Lw[:], r_L[:]),
                         waits=[("dve", v)], inc="dve")
                for rnd in range(3):
                    v = P.op("vector", lambda e: e.reduce_max(
                        r_es[:], r_Lw[:], axis=AX.X, negate=True),
                        waits=[("dve", v)], inc="dve")
                    if rnd == 0:
                        v = P.op("vector", lambda e: e.tensor_copy(r_m1[:], r_es[:]),
                                 waits=[("dve", v)], inc="dve")
                    v = P.op("vector", lambda e: e.tensor_add(
                        r_ge[:], r_Lw[:], r_es[:].to_broadcast([128, NE])),
                        waits=[("dve", v)], inc="dve")
                    v = P.op("vector", lambda e: e.tensor_single_scalar(
                        r_ge[:], r_ge[:], 0.0, ALU.is_ge),
                        waits=[("dve", v)], inc="dve")
                    v = P.op("vector", lambda e: e.scalar_tensor_tensor(
                        r_Lw[:], r_ge[:], -1e30, r_Lw[:], ALU.mult, ALU.add),
                        waits=[("dve", v)], inc="dve")
                v = P.op("vector", lambda e: e.reduce_max(
                    r_th[:], r_Lw[:], axis=AX.X, negate=True),
                    waits=[("dve", v)], inc="dve")
                v = P.op("vector", lambda e: e.tensor_add(
                    r_ge[:], r_L[:], r_th[:].to_broadcast([128, NE])),
                    waits=[("dve", v)], inc="dve")
                v = P.op("vector", lambda e: e.tensor_single_scalar(
                    r_ge[:], r_ge[:], 0.0, ALU.is_ge),
                    waits=[("dve", v)], inc="dve")
                a = P.op("scalar", lambda e: e.activation(
                    r_wft[:], r_L[:], AF.Exp, bias=r_m1[:]),
                    waits=[("dve", v)], inc="act")
                v = P.op("vector", lambda e: e.tensor_mul(
                    r_wft[:], r_wft[:], r_ge[:]),
                    waits=[("act", a)], inc="dve")
                v = P.op("vector", lambda e: e.reduce_sum(
                    r_wsum[:], r_wft[:], axis=AX.X),
                    waits=[("dve", v)], inc="dve")
                v = P.op("vector", lambda e: e.reciprocal(
                    r_es[:], r_wsum[:]), waits=[("dve", v)], inc="dve")
                v = P.op("vector", lambda e: e.tensor_mul(
                    r_wft[:], r_wft[:], r_es[:].to_broadcast([128, NE])),
                    waits=[("dve", v)], inc="dve")
                pe_t = P.op("tensor", lambda e: e.transpose(
                    psT[:, :], r_wft[:], c_idf[:]),
                    waits=[("dve", v)], inc="pe")
                v = P.op("vector", lambda e, it=it: e.tensor_copy(
                    r_wffm[:, it * 128:(it + 1) * 128], psT[:, :]),
                    waits=[("pe", pe_t)], inc="dve")
            P.dma("sync", wfscr, r_wffm[:], "dio",
                  waits=[("dve", K.cnt["dve"])])
            if debug:
                P.dma("sync", dbg["d_x2"], x2_sb[:], "dio",
                      waits=[("dve", K.cnt["dve"])])
                P.dma("sync", dbg["d_wf"], r_wffm[:], "dio",
                      waits=[("dve", K.cnt["dve"])])
            P.run(nc, "p5c")

        if upto == "p5c":

            return nc, dbg

        with ExitStack() as pc:
            psGa = pc.enter_context(nc.psum_tensor("psGa", [128, 512], F32))
            psGb = pc.enter_context(nc.psum_tensor("psGb", [128, 512], F32))
            psUa = pc.enter_context(nc.psum_tensor("psUa", [128, 512], F32))
            psUb = pc.enter_context(nc.psum_tensor("psUb", [128, 512], F32))
            psDa = pc.enter_context(nc.psum_tensor("psDa", [128, 512], F32))
            psDb = pc.enter_context(nc.psum_tensor("psDb", [128, 512], F32))
            W1ST = {"wi": 0}
            W2ST = {"wi": 0}

            def gu_block(P, w_src_ap_fn, with_wb, wb_rdy, pe_guard):
                dve_fin = 0
                for gt in range(NIT):
                    wi = W1ST["wi"]
                    wb = w1A if wi % 2 == 0 else w1B
                    wsem = "dwA" if wi % 2 == 0 else "dwB"
                    wwaits = [("pe", W1ST.get(("free", wi - 2), 0))]
                    wld = P.dma("gpsimd",
                                ap2(wb, 0, [[4096, 128], [2048, 2], [128, DT], [1, 128]]),
                                w_src_ap_fn(gt), wsem, waits=wwaits)
                    for f in range(2):
                        psGx = psGa if f == 0 else psGb
                        psUx = psUa if f == 0 else psUb
                        mmw = [(wsem, wld)]
                        if wb_rdy:
                            mmw.append(("dve", wb_rdy))
                        pk = ("guev", wi - 1, f)
                        if pk in W1ST:
                            mmw.append(W1ST[pk])
                        for dt in range(DT):
                            P.op("tensor", lambda e, dt=dt, f=f, psGx=psGx, wb=wb: e.matmul(
                                psGx[:, :], wb[:, dt * 128:(dt + 1) * 128],
                                x2_sb[:, dt * TOKO + f * 512: dt * TOKO + f * 512 + 512],
                                start=(dt == 0), stop=(dt == DT - 1)),
                                waits=mmw if dt == 0 else [])
                        for dt in range(DT):
                            P.op("tensor", lambda e, dt=dt, f=f, psUx=psUx, wb=wb: e.matmul(
                                psUx[:, :], wb[:, 2048 + dt * 128:2048 + dt * 128 + 128],
                                x2_sb[:, dt * TOKO + f * 512: dt * TOKO + f * 512 + 512],
                                start=(dt == 0), stop=(dt == DT - 1)),
                                inc="pe" if dt == DT - 1 else None)
                        pe_c = K.cnt["pe"]
                        W1ST[("free", wi)] = pe_c
                        a = P.op("scalar", lambda e, psGx=psGx: e.activation(
                            r_tsil[:, :], psGx[:, :], AF.Silu),
                            waits=[("pe", pe_c)], inc="act")
                        awaits = [("act", a)]
                        if gt == 0 and f == 0:
                            awaits.append(("pe", pe_guard))
                        if with_wb:
                            v = P.op("vector", lambda e, f=f: e.tensor_mul(
                                r_tsil2[:, :], r_tsil[:, :],
                                r_wb[:, f * 512:(f + 1) * 512]),
                                waits=awaits, inc="dve")
                            v = P.op("vector", lambda e, gt=gt, f=f, psUx=psUx: e.tensor_mul(
                                acts_sb[:, gt * TOKO + f * 512: gt * TOKO + f * 512 + 512],
                                r_tsil2[:, :], psUx[:, :]),
                                waits=[("dve", v)], inc="dve")
                        else:
                            v = P.op("vector", lambda e, gt=gt, f=f, psUx=psUx: e.tensor_mul(
                                acts_sb[:, gt * TOKO + f * 512: gt * TOKO + f * 512 + 512],
                                r_tsil[:, :], psUx[:, :]),
                                waits=awaits, inc="dve")
                        W1ST[("guev", wi, f)] = ("dve", v)
                        dve_fin = v
                    W1ST["wi"] = wi + 1
                return dve_fin

            def down_block(P, w_dram, row0_elems, acts_rdy):
                for hc in range(8):
                    wi = W2ST["wi"]
                    wb = w2A if wi % 2 == 0 else w2B
                    wsem = "dwA" if wi % 2 == 0 else "dwB"
                    wwaits = [("pe", W2ST.get(("free", wi - 2), 0))]
                    wld = P.dma("gpsimd", wb[:],
                                ap2(w_dram, row0_elems + hc * 256,
                                    [[D, 128], [128 * D, NIT], [1, 256]]),
                                wsem, waits=wwaits)
                    for o in range(2):
                        for f in range(2):
                            idx = hc * 4 + o * 2 + f
                            ps = psDa if idx % 2 == 0 else psDb
                            mmw = [(wsem, wld), ("dve", acts_rdy)]
                            pk = ("dev", W2ST["blk"], idx - 2)
                            if pk in W2ST:
                                mmw.append(W2ST[pk])
                            for it in range(NIT):
                                P.op("tensor", lambda e, it=it, o=o, ps=ps, wb=wb, f=f: e.matmul(
                                    ps[:, :],
                                    wb[:, it * 256 + o * 128: it * 256 + o * 128 + 128],
                                    acts_sb[:, it * TOKO + f * 512: it * TOKO + f * 512 + 512],
                                    start=(it == 0), stop=(it == NIT - 1)),
                                    waits=mmw if it == 0 else [],
                                    inc="pe" if it == NIT - 1 else None)
                            pe_c = K.cnt["pe"]
                            W2ST[("free", wi)] = pe_c
                            ht = hc * 2 + o
                            v = P.op("vector", lambda e, ps=ps, ht=ht, f=f: e.tensor_add(
                                hid_sb[:, ht * TOKO + f * 512: ht * TOKO + f * 512 + 512],
                                ps[:, :],
                                hid_sb[:, ht * TOKO + f * 512: ht * TOKO + f * 512 + 512]),
                                waits=[("pe", pe_c)], inc="dve")
                            W2ST[("dev", W2ST["blk"], idx)] = ("dve", v)
                    W2ST["wi"] = wi + 1

            P = Plan(K)
            for H in range(2):
                def shsrc(gt, H=H):
                    return ap2(shgu_t, (H * NIT + gt) * 128,
                               [[2 * SHI, 128], [SHI, 2], [128 * 2 * SHI, DT], [1, 128]])
                ardy = gu_block(P, shsrc, False, 0, K.cnt["pe"])
                W2ST["blk"] = H
                down_block(P, shdn_t, (H * NIT * 128) * D, ardy)
            P.run(nc, "p6a")

            if upto == "p6a":

                return nc, dbg

            psW = pc.enter_context(nc.psum_tensor("psW", [128, 512], F32))
            P = Plan(K)
            wrow_guard = 0
            vcp = 0
            for e_i in range(NE):
                wr = P.dma("gpsimd", r_wrow[:],
                           ap2(wfscr, e_i * TOKO, [[TOKO, 1], [1, TOKO]]),
                           "dwA",
                           waits=[("pe", wrow_guard), ("dio", K.cnt["dio"])])
                for f in range(2):
                    pb = P.op("tensor", lambda e, f=f: e.matmul(
                        psW[:, :], c_orb[:], r_wrow[0:1, f * 512:(f + 1) * 512],
                        start=True, stop=True),
                        waits=[("dwA", wr), ("dve", vcp)], inc="pe")
                    vcp = P.op("vector", lambda e, f=f: e.tensor_copy(
                        r_wb[:, f * 512:(f + 1) * 512], psW[:, :]),
                        waits=[("pe", pb)], inc="dve")
                    wrow_guard = pb

                def w1src(gt, e_i=e_i):
                    return ap2(w1_t, e_i * D * 2 * I_FF + gt * 128,
                               [[2 * I_FF, 128], [I_FF, 2],
                                [128 * 2 * I_FF, DT], [1, 128]])
                ardy = gu_block(P, w1src, True, vcp, K.cnt["pe"])
                W2ST["blk"] = 2 + e_i
                down_block(P, w2_t, e_i * I_FF * D, ardy)
                if debug and e_i == 0:
                    P.dma("sync", dbg["d_act0"], acts_sb[:], "dio",
                          waits=[("dve", ardy)])
            P.run(nc, "p6b")
            if upto == "p6b":
                return nc, dbg

        with ExitStack() as pc:
            P = Plan(K)
            for dt in range(DT):
                for f in range(2):
                    P.op("scalar", lambda e, dt=dt, f=f: e.activation(
                        g2_sb[:, dt * TOKO + f * 512: dt * TOKO + f * 512 + 512],
                        hid_sb[:, dt * TOKO + f * 512: dt * TOKO + f * 512 + 512],
                        AF.Silu),
                        inc="act" if (dt == DT - 1 and f == 1) else None)
            a_g2 = K.cnt["act"]
            if debug:
                P.dma("sync", dbg["d_ffo"], hid_sb[:], "dio", waits=[])
            stored = {}
            for sp in range(8):
                xb7 = xo7A if sp % 2 == 0 else xo7B
                ob7 = out7s[sp % 2]
                lsem = "kA" if sp % 2 == 0 else "kB"
                osem = "sA" if sp % 2 == 0 else "sB"
                lw = [("dve", stored[sp - 2])] if sp >= 2 else []
                ld = P.dma("sync", xb7[:],
                           ap2(xoT, sp * 128,
                               [[TOKO, 128], [128 * TOKO, DT], [1, 128]]),
                           lsem, waits=lw)
                for dt in range(DT):
                    v = P.op("vector", lambda e, dt=dt, sp=sp: e.tensor_mul(
                        r_tsil[:, 0:128],
                        s_sb[:, dt * TOKO + sp * 128: dt * TOKO + sp * 128 + 128],
                        g2_sb[:, dt * TOKO + sp * 128: dt * TOKO + sp * 128 + 128]),
                        waits=[("act", a_g2), (lsem, ld),
                               (osem, K.cnt[osem])],
                        inc="dve")
                    v = P.op("vector", lambda e, dt=dt, xb7=xb7, ob7=ob7: e.tensor_add(
                        ob7[:, dt * 128:(dt + 1) * 128],
                        xb7[:, dt * 128:(dt + 1) * 128], r_tsil[:, 0:128]),
                        waits=[("dve", v)],
                        inc="dve" if dt == DT - 1 else None)
                stored[sp] = K.cnt["dve"]
                P.dma("sync", ap2(outT, sp * 128,
                                  [[TOKO, 128], [128 * TOKO, DT], [1, 128]]),
                      ob7[:], osem, waits=[("dve", stored[sp])])
            P.op("sync", lambda e: e.nop(),
                 waits=[("sA", K.cnt["sA"]), ("sB", K.cnt["sB"])])
            P.run(nc, "p7")

    if upto == "p7":

        return nc, dbg

    return nc, dbg


_NC_CACHE = {}


def _get_nc(debug=False, upto=None):
    key = (debug, upto)
    if key not in _NC_CACHE:
        _NC_CACHE[key] = build(debug, upto)
    return _NC_CACHE[key]


def make_inputs(x_np, weights):
    f32 = np.float32
    wq = np.ascontiguousarray(np.asarray(weights["L0_attn_Wq"]).T, dtype=f32)
    wk = np.ascontiguousarray(np.asarray(weights["L0_attn_Wk"]).T, dtype=f32)
    wv = np.ascontiguousarray(np.asarray(weights["L0_attn_Wv"]).T, dtype=f32)
    wg = np.ascontiguousarray(np.asarray(weights["L0_attn_Wg"]).T, dtype=f32)
    wp = np.ascontiguousarray(np.asarray(weights["L0_attn_Wp"]).T, dtype=f32)
    qw = np.asarray(weights["L0_ffn_q_w"])
    qsel = np.concatenate([np.arange(h * 192, h * 192 + 128) for h in range(NH)])
    qw_nope = np.ascontiguousarray(qw[qsel].T, dtype=f32)
    kvw = np.ascontiguousarray(np.asarray(weights["L0_ffn_kv_w"])[:2 * D].T, dtype=f32)
    ow = np.ascontiguousarray(np.asarray(weights["L0_ffn_o_w"]).T, dtype=f32)
    gw = np.ascontiguousarray(np.asarray(weights["L0_ffn_gate_w"]).T, dtype=f32)
    w1 = np.ascontiguousarray(np.transpose(np.asarray(weights["L0_ffn_w1"]), (0, 2, 1)), dtype=f32)
    w2 = np.ascontiguousarray(np.transpose(np.asarray(weights["L0_ffn_w2"]), (0, 2, 1)), dtype=f32)
    shgu = np.ascontiguousarray(np.asarray(weights["L0_ffn_sh_gu_w"]).T, dtype=f32)
    shdn = np.ascontiguousarray(np.asarray(weights["L0_ffn_sh_down_w"]).T, dtype=f32)
    ln1 = np.ascontiguousarray(
        np.asarray(weights["L0_ffn_ln1_w"], dtype=f32).reshape(DT, 128).T)
    ln2 = np.ascontiguousarray(
        np.asarray(weights["L0_ffn_ln2_w"], dtype=f32).reshape(DT, 128).T)

    tri_ji = np.triu(np.ones((128, 128), dtype=f32))
    tril_add = np.where(np.tril(np.ones((128, 128))), 0.0, -1e9).astype(f32)
    ones128 = np.ones((128, 128), dtype=f32)
    zeros128 = np.zeros((128, 128), dtype=f32)
    fullneg = np.full((128, 128), -1e9, dtype=f32)
    shared = dict(
        wq_t=wq, wk_t=wk, wv_t=wv, wg_t=wg, wp_t=wp, qw_t=qw_nope,
        kvw_t=kvw, ow_t=ow, gw_t=gw, w1_t=w1, w2_t=w2,
        shgu_t=shgu, shdn_t=shdn, ln1_r=ln1, ln2_r=ln2,
        identb=np.eye(128, dtype=bfloat16),
        identf=np.eye(128, dtype=np.float32),
        ones_colf=np.ones((128, 1), dtype=f32),
        ones_colb=np.ones((128, 1), dtype=bfloat16),
        ones_rowb=np.ones((1, 128), dtype=bfloat16),
        ones_rowf=np.ones((1, 128), dtype=f32),
        epsv=np.broadcast_to(np.array([EPS_LN, EPS_RMS2, EPS_LN1, EPS_LN2],
                                      dtype=f32), (128, 4)).copy(),
    )
    in_maps = []
    idxs = []
    for c in range(8):
        b, p = c // 2, c % 2
        idx = np.concatenate([np.arange((2 * s + p) * 128, (2 * s + p) * 128 + 128)
                              for s in range(NSLOT)])
        idxs.append((b, idx))
        m = dict(shared)
        m["xfT"] = np.ascontiguousarray(x_np[b].T, dtype=f32)
        m["xoT"] = np.ascontiguousarray(x_np[b][idx].T, dtype=f32)
        if p == 0:
            b0m, b1m = tril_add, fullneg
            lm0, lm1 = tri_ji, zeros128
        else:
            b0m, b1m = zeros128, tril_add
            lm0, lm1 = ones128, tri_ji
        m["la_m0w"] = np.concatenate(
            [lm0, np.ones((128, 384), dtype=f32)], axis=1).astype(bfloat16)
        m["la_m1w"] = np.concatenate(
            [lm1, np.ones((128, 384), dtype=f32)], axis=1).astype(bfloat16)
        m["ff_w512"] = np.concatenate(
            [np.zeros((128, 256), dtype=f32), b0m, b1m], axis=1).astype(bfloat16)
        m["ff_w256"] = np.concatenate([b0m, b1m], axis=1).astype(bfloat16)
        in_maps.append(m)
    return in_maps, idxs


def kernel(**inputs):
    x = np.asarray(inputs["L0_attn_x"], dtype=np.float32)
    nc, _dbg = _get_nc(False)
    in_maps, idxs = make_inputs(x, inputs)
    res = run_bass_kernel_spmd(nc, in_maps, core_ids=list(range(8)))
    out = np.empty((4, TOKF, D), dtype=np.float32)
    for c in range(8):
        b, idx = idxs[c]
        out[b][idx] = np.asarray(res.results[c]["outT"]).T
    return out


# version 12 (final)
# speedup vs baseline: 1.1082x; 1.1082x over previous
import numpy as np
import ml_dtypes
import concourse.bass as bass
import concourse.mybir as mybir
from concourse.bass_utils import run_bass_kernel_spmd
from contextlib import ExitStack

F32 = mybir.dt.float32
BF16 = mybir.dt.bfloat16
AF = mybir.ActivationFunctionType
ALU = mybir.AluOpType
AX = mybir.AxisListType
bfloat16 = ml_dtypes.bfloat16

D = 2048
DT = 16
TOKF = 2048
TOKO = 1024
NSLOT = 8
NH = 16
NE = 16
I_FF = 1408
NIT = 11
SHI = 2816
SCALE_ATTN = float(192.0 ** -0.5)
EPS_LN = 1e-5
EPS_RMS2 = 1.1920929e-07
EPS_LN1 = 1e-6
EPS_LN2 = 1e-6

ENGS = ("sync", "gpsimd", "tensor", "scalar", "vector")
P4A_JMAX = 16
P4A_NOMASK = False
P4A_MASKVAR = 0
P4A_ONECHUNK = False
P4A_ONEBUF = False


class Plan:
    def __init__(self, K):
        self.K = K
        self.streams = {e: [] for e in ENGS}
        eng_sems = ("pe", "act", "dve", "gp")
        self.guar = {e: {s: v for s, v in K.guar.items() if s in eng_sems}
                     for e in ENGS}
        self._seen = set()

    def op(self, eng, fn, waits=(), inc=None, inc_n=1):
        K = self.K
        if eng not in self._seen and getattr(K, "const_waits", None):
            waits = list(waits) + K.const_waits
        self._seen.add(eng)
        incs = []
        if inc is not None:
            K.cnt[inc] += inc_n
            incs = [(K.sems[inc], inc_n)]
        wl = []
        g = self.guar[eng]
        for s, v in waits:
            if v > g.get(s, 0):
                wl.append((K.sems[s], v))
                g[s] = v

        def emit(engine, wl=wl, fn=fn, incs=incs):
            for sem, v in wl:
                engine.wait_ge(sem, v)
            inst = fn(engine)
            for sem, n in incs:
                inst.then_inc(sem, n)

        self.streams[eng].append(emit)
        return K.cnt[inc] if inc is not None else None

    def dma(self, eng, out, in_, sem, waits=()):
        return self.op(eng, lambda e, o=out, i=in_: e.dma_start(out=o, in_=i),
                       waits=waits, inc=sem, inc_n=16)

    def run(self, nc, name):
        with nc.Block(name) as block:
            for eng_name in ENGS:
                ops = self.streams[eng_name]
                if not ops:
                    continue

                def section(engine, ops=ops):
                    for o in ops:
                        o(engine)

                getattr(block, eng_name)(section)
        self.K.guar = dict(self.K.cnt)


class KCtx:
    def __init__(self, nc, stack):
        self.nc = nc
        self.sems = {}
        self.cnt = {}
        self.guar = {}
        for n in ["pe", "act", "dve", "gp", "dio", "dwA", "dwB",
                  "kA", "kB", "kC", "vA", "vB", "vC", "sA", "sB"]:
            self.sems[n] = stack.enter_context(nc.semaphore(n))
            self.cnt[n] = 0

    def at(self, name, shape, dtype, off):
        return self.nc.alloc_sbuf_tensor_at(name, shape, dtype, offset=off)


def ap2(t, offset, dims):
    h = t.tensor if isinstance(t, bass.AP) else t
    return bass.AP(h, offset, [list(d) for d in dims])


def build(debug=False, upto=None):
    nc = bass.Bass("TRN2", target_bir_lowering=False)

    def din(name, shape, dt=F32):
        return nc.dram_tensor(name, list(shape), dt, kind="ExternalInput").ap()

    xoT = din("xoT", (D, TOKO))
    xfT = din("xfT", (D, TOKF))
    wq_t = din("wq_t", (D, D))
    wk_t = din("wk_t", (D, D))
    wv_t = din("wv_t", (D, D))
    wg_t = din("wg_t", (D, D))
    wp_t = din("wp_t", (D, D))
    qw_t = din("qw_t", (D, D))
    kvw_t = din("kvw_t", (D, 2 * D))
    ow_t = din("ow_t", (D, D))
    gw_t = din("gw_t", (D, NE))
    w1_t = din("w1_t", (NE, D, 2 * I_FF))
    w2_t = din("w2_t", (NE, I_FF, D))
    shgu_t = din("shgu_t", (D, 2 * SHI))
    shdn_t = din("shdn_t", (SHI, D))
    ln1_r = din("ln1_r", (128, DT))
    ln2_r = din("ln2_r", (128, DT))
    la_m0w = din("la_m0w", (128, 512), BF16)
    la_m1w = din("la_m1w", (128, 512), BF16)
    ff_w512 = din("ff_w512", (128, 512), BF16)
    ff_w256 = din("ff_w256", (128, 256), BF16)
    identb = din("identb", (128, 128), BF16)
    identf = din("identf", (128, 128), F32)
    ones_colf = din("ones_colf", (128, 1), F32)
    ones_colb = din("ones_colb", (128, 1), BF16)
    ones_rowb = din("ones_rowb", (1, 128), BF16)
    ones_rowf = din("ones_rowf", (1, 128), F32)
    epsv = din("epsv", (128, 4), F32)

    outT = nc.dram_tensor("outT", [D, TOKO], F32, kind="ExternalOutput").ap()

    skind = "ExternalOutput" if debug else "Internal"
    laK = nc.dram_tensor("laK", [D, TOKF], BF16, kind=skind).ap()
    laV = nc.dram_tensor("laV", [TOKF, D], BF16, kind=skind).ap()
    fK = nc.dram_tensor("fK", [D, TOKF], BF16, kind=skind).ap()
    fV = nc.dram_tensor("fV", [TOKF, D], BF16, kind=skind).ap()
    wfscr = nc.dram_tensor("wfscr", [NE, TOKO], F32, kind=skind).ap()

    dbg = {}
    if debug:
        for nm, shape, dt in [("d_xn", (128, DT * TOKF), BF16),
                              ("d_xr", (128, DT * TOKF), BF16),
                              ("d_x1", (128, DT * TOKF), BF16),
                              ("d_q", (128, DT * TOKO), BF16),
                              ("d_g", (128, DT * TOKO), BF16),
                              ("d_fq", (128, DT * TOKO), BF16),
                              ("d_xno", (128, DT * TOKO), BF16),
                              ("d_st", (128, DT * TOKO), BF16),
                              ("d_og", (128, DT * TOKO), BF16),
                              ("d_s", (128, DT * TOKO), BF16),
                              ("d_attn", (128, DT * TOKO), BF16),
                              ("d_hid", (128, DT * TOKO), F32),
                              ("d_x2", (128, DT * TOKO), BF16),
                              ("d_wf", (NE, TOKO), F32),
                              ("d_act0", (128, NIT * TOKO), BF16),
                              ("d_ffo", (128, DT * TOKO), F32)]:
            dbg[nm] = nc.dram_tensor(nm, list(shape), dt, kind="ExternalOutput").ap()

    stack = ExitStack()
    with stack:
        K = KCtx(nc, stack)
        probe_t = nc.alloc_sbuf_tensor("probe_base", [128, 16], F32)
        BASE = (nc.lookup_mloc(probe_t).addr + 64 + 63) // 64 * 64
        LIMIT = 229344
        C0 = BASE
        R0 = C0 + 6144
        A1 = R0 + 12288
        A2 = A1 + 65536
        A3 = A2 + 65536
        A4 = A3 + 32768
        assert A4 + 27648 <= LIMIT, (BASE, A4, LIMIT)

        at = K.at
        c_lam0 = at("c_lam0", [128, 512], BF16, C0 + 0)
        c_lam1 = at("c_lam1", [128, 512], BF16, C0 + 1024)
        c_ffw512 = at("c_ffw512", [128, 512], BF16, C0 + 2048)
        c_ffw256 = at("c_ffw256", [128, 256], BF16, C0 + 3072)
        c_id = at("c_id", [128, 128], BF16, C0 + 3584)
        c_idf = at("c_idf", [128, 128], F32, C0 + 3840)
        c_ocf = at("c_ocf", [128, 1], F32, C0 + 4352)
        c_ocb = at("c_ocb", [128, 1], BF16, C0 + 5888)
        c_orb = at("c_orb", [1, 128], BF16, C0 + 4416)
        c_orf = at("c_orf", [1, 128], F32, C0 + 4672)
        c_ln1 = at("c_ln1", [128, DT], F32, C0 + 5184)
        c_ln2 = at("c_ln2", [128, DT], F32, C0 + 5248)
        c_gwb = at("c_gwb", [128, DT * NE], BF16, C0 + 5312)
        c_eps = at("c_eps", [128, 4], F32, C0 + 5824)
        nc.const_aps.register(F32, EPS_LN, c_eps[:, 0:1])
        nc.const_aps.register(F32, EPS_RMS2, c_eps[:, 1:2])
        nc.const_aps.register(F32, EPS_LN1, c_eps[:, 2:3])
        nc.const_aps.register(F32, EPS_LN2, c_eps[:, 3:4])

        r_mean = at("r_mean", [1, 512], F32, R0 + 0)
        r_msq = at("r_msq", [1, 512], F32, R0 + 2048)
        r_var = at("r_var", [1, 512], F32, R0 + 4096)
        r_rstd = at("r_rstd", [1, 512], F32, R0 + 6144)
        r2o = at("r2o", [1, TOKO], F32, A4 + 16384)
        r1o = at("r1o", [1, TOKO], F32, A4 + 20480)
        r_mrows = [at("r_mrow0", [128, 1], F32, R0 + 0),
                   at("r_mrow1", [128, 1], F32, R0 + 512)]
        r_brows = [at("r_brow0", [128, 1], F32, R0 + 64),
                   at("r_brow1", [128, 1], F32, R0 + 576)]
        r_dens = [at("r_den0", [128, 1], F32, R0 + 128),
                  at("r_den1", [128, 1], F32, R0 + 640)]
        r_rdens = [at("r_rden0", [128, 1], F32, R0 + 192),
                   at("r_rden1", [128, 1], F32, R0 + 704)]
        r_L = at("r_L", [128, NE], F32, R0 + 1024)
        r_Lw = at("r_Lw", [128, NE], F32, R0 + 1088)
        r_ge = at("r_ge", [128, NE], F32, R0 + 1152)
        r_m1 = at("r_m1", [128, 1], F32, R0 + 1216)
        r_th = at("r_th", [128, 1], F32, R0 + 1280)
        r_es = at("r_es", [128, 1], F32, R0 + 1344)
        r_wsum = at("r_wsum", [128, 1], F32, R0 + 1408)
        r_wft = at("r_wft", [128, NE], F32, R0 + 1472)
        r_tsil = at("r_tsil", [128, 512], BF16, R0 + 2048)
        r_tsil2 = at("r_tsil2", [128, 512], BF16, R0 + 3072)
        r_xtmp = at("r_xtmp", [128, 512], F32, R0 + 4096)
        r_wrow = at("r_wrow", [1, TOKO], BF16, R0 + 8192)
        r_wb = at("r_wb", [128, TOKO], BF16, R0 + 10240)

        xn_full = at("xn_full", [128, DT * TOKF], BF16, A1)
        xbufA = at("xbufA", [128, DT * 256], F32, A2)
        xbufB = at("xbufB", [128, DT * 256], F32, A2 + 16384)
        sq_sp = at("sq_sp", [128, DT * 256], F32, A2 + 32768)
        xr_full = at("xr_full", [128, DT * TOKF], BF16, A2)
        ebufA = at("ebufA", [128, 512], BF16, A3 + 0)
        ebufB = at("ebufB", [128, 512], BF16, A3 + 1024)
        r2_all = at("r2_all", [1, TOKF], F32, A3 + 16384)
        r1_all = at("r1_all", [1, TOKF], F32, A3 + 24576)

        xn_own = at("xn_own", [128, DT * TOKO], BF16, A1)
        q_sb = at("q_sb", [128, DT * TOKO], BF16, A1 + 32768)
        g_sb = at("g_sb", [128, DT * TOKO], BF16, A2)
        fq_sb = at("fq_sb", [128, DT * TOKO], BF16, A2 + 32768)
        xobufA = at("xobufA", [128, DT * 256], F32, A3)
        xobufB = at("xobufB", [128, DT * 256], F32, A3 + 16384)
        sq_o = at("sq_o", [128, DT * 256], F32, A4 + 0)
        xro = at("xro", [128, DT * TOKO], BF16, A3)
        wbA = at("wbA", [128, DT * 128], BF16, A4 + 0)
        wbB = at("wbB", [128, DT * 128], BF16, A4 + 4096)
        wvA = at("wvA", [128, DT * 256], BF16, A4 + 8192)
        wvB = at("wvB", [128, DT * 256], BF16, A4 + 16384)

        st_sb = at("st_sb", [128, DT * TOKO], BF16, A3)
        og_sb = at("og_sb", [128, DT * TOKO], BF16, A1 + 32768)
        s_sb = at("s_sb", [128, DT * TOKO], BF16, A3)
        kjA = at("kjA", [128, DT * 128], BF16, A4 + 0)
        kjB = at("kjB", [128, DT * 128], BF16, A4 + 4096)
        vlA = at("vlA", [128, 512], BF16, A4 + 8192)
        vlB = at("vlB", [128, 512], BF16, A4 + 9216)
        wpA = at("wpA", [128, DT * 128], BF16, A4 + 10240)
        wpB = at("wpB", [128, DT * 128], BF16, A4 + 14336)

        vbandA = at("vbandA", [128, DT * 256], BF16, A4 + 0)
        vbandB = at("vbandB", [128, DT * 256], BF16, A4 + 8192)
        kbs = [at(f"kb{i}", [128, TOKF], BF16, A4 + i * 4096) for i in range(3)]
        vbs = [at(f"vb{i}", [128, TOKF], BF16, A4 + 12288 + i * 4096)
               for i in range(3)]
        owA = at("owA", [128, DT * 128], BF16, A4 + 0)
        owB = at("owB", [128, DT * 128], BF16, A4 + 4096)
        S_sbs = [at("S_sb0", [128, TOKF], F32, A2),
                 at("S_sb1", [128, TOKF], F32, A2 + 16384)]
        P_sbs = [at("P_sb0", [128, TOKF], BF16, A2 + 8192),
                 at("P_sb1", [128, TOKF], BF16, A2 + 24576)]
        Pt_sbs = [at("Pt_sb0", [128, TOKF], BF16, A2 + 12288),
                  at("Pt_sb1", [128, TOKF], BF16, A2 + 28672)]
        attn_sb = at("attn_sb", [128, DT * TOKO], BF16, A1 + 32768)
        hid_sb = at("hid_sb", [128, DT * TOKO], F32, A2)
        x2_sb = at("x2_sb", [128, DT * TOKO], BF16, A1)
        p5sqA = at("p5sqA", [128, 512], F32, A4 + 0)
        p5sqB = at("p5sqB", [128, 512], F32, A4 + 2048)
        r_wffm = at("r_wffm", [16, TOKO], F32, A4 + 4096)

        acts_sb = at("acts_sb", [128, NIT * TOKO], BF16, A1 + 32768)
        w1A = at("w1A", [128, 2 * DT * 128], BF16, A4 + 0)
        w1B = at("w1B", [128, 2 * DT * 128], BF16, A4 + 8192)
        w2A = at("w2A", [128, NIT * 256], BF16, A4 + 16384)
        w2B = at("w2B", [128, NIT * 256], BF16, A4 + 22016)

        g2_sb = at("g2_sb", [128, DT * TOKO], BF16, A1)
        xo7A = at("xo7A", [128, DT * 128], F32, A4 + 0)
        xo7B = at("xo7B", [128, DT * 128], F32, A4 + 8192)
        out7 = at("out7", [128, DT * 128], F32, A4 + 16384)

        P = Plan(K)
        for dst, src in [(c_lam0, la_m0w), (c_lam1, la_m1w),
                         (c_ffw512, ff_w512), (c_ffw256, ff_w256),
                         (c_id, identb), (c_idf, identf),
                         (c_ocf, ones_colf), (c_ocb, ones_colb),
                         (c_orb, ones_rowb),
                         (c_orf, ones_rowf), (c_ln1, ln1_r), (c_ln2, ln2_r),
                         (c_eps, epsv)]:
            P.dma("sync", dst[:], src, "dio")
        P.dma("gpsimd", c_gwb[:],
              ap2(gw_t, 0, [[NE, 128], [128 * NE, DT], [1, NE]]), "dwA")
        P.run(nc, "p0")
        K.const_waits = [("dio", K.cnt["dio"]), ("dwA", K.cnt["dwA"])]

        if upto == "p0":

            return nc, dbg


        def emit_ln_phase(P, src_dram, n_tok, xn_dst, r2_dst, r1_dst,
                          bufs, sqb, psS, psQ, psBr, psBm):
            W = 256
            nsp = n_tok // W
            prev = {"v_xn": {}, "pe_s": {}, "pe_q2": {}, "v_rows": {}}
            lds = {}
            spandone = {}

            def issue_load(f):
                if f >= nsp:
                    return
                xb = bufs[f % 2]
                lsem = "kA" if f % 2 == 0 else "kB"
                lw = []
                if f >= 2:
                    lw = [("dve", prev["v_xn"][f - 2]), ("pe", prev["pe_s"][f - 2])]
                lds[f] = P.dma("sync", xb[:],
                               ap2(src_dram, f * W,
                                   [[n_tok, 128], [128 * n_tok, DT], [1, W]]),
                               lsem, waits=lw)

            issue_load(0)
            issue_load(1)
            for f in range(nsp):
                xb = bufs[f % 2]
                lsem = "kA" if f % 2 == 0 else "kB"
                ld = lds[f]
                for dt in range(DT):
                    P.op("vector", lambda e, dt=dt, xb=xb: e.tensor_mul(
                        sqb[:, dt * W:(dt + 1) * W],
                        xb[:, dt * W:(dt + 1) * W], xb[:, dt * W:(dt + 1) * W]),
                        waits=[(lsem, ld), ("pe", prev["pe_q2"].get(f - 1, 0))],
                        inc="dve" if dt == DT - 1 else None)
                v_sq = K.cnt["dve"]
                for dt in range(DT):
                    P.op("tensor", lambda e, dt=dt, xb=xb: e.matmul(
                        psS[:, :], c_ocf[:], xb[:, dt * W:(dt + 1) * W],
                        start=(dt == 0), stop=(dt == DT - 1)),
                        waits=[(lsem, ld), ("dve", prev["v_rows"].get(f - 1, 0))],
                        inc="pe" if dt == DT - 1 else None)
                pe_s = K.cnt["pe"]
                prev["pe_s"][f] = pe_s
                for dt in range(DT):
                    P.op("tensor", lambda e, dt=dt: e.matmul(
                        psQ[:, :], c_ocf[:], sqb[:, dt * W:(dt + 1) * W],
                        start=(dt == 0), stop=(dt == DT - 1)),
                        waits=[("dve", v_sq)],
                        inc="pe" if dt == DT - 1 else None)
                pe_q = K.cnt["pe"]
                v = P.op("vector", lambda e: e.tensor_scalar_mul(
                    r_mean[:, 0:W], psS[0:1, :], 1.0 / D),
                    waits=[("pe", pe_s)], inc="dve")
                v = P.op("vector", lambda e: e.tensor_scalar_mul(
                    r_msq[:, 0:W], psQ[0:1, :], 1.0 / D),
                    waits=[("pe", pe_q)], inc="dve")
                v = P.op("vector", lambda e: e.tensor_mul(
                    r_var[:, 0:W], r_mean[:, 0:W], r_mean[:, 0:W]),
                    waits=[("dve", v)], inc="dve")
                v = P.op("vector", lambda e: e.tensor_sub(
                    r_var[:, 0:W], r_msq[:, 0:W], r_var[:, 0:W]),
                    waits=[("dve", v)], inc="dve")
                a = P.op("scalar", lambda e: e.activation(
                    r_msq[:, 0:W], r_var[:, 0:W], AF.Ln, bias=EPS_LN),
                    waits=[("dve", v)], inc="act")
                a = P.op("scalar", lambda e: e.activation(
                    r_rstd[:, 0:W], r_msq[:, 0:W], AF.Exp, scale=-0.5),
                    waits=[("act", a)], inc="act")
                v = P.op("vector", lambda e: e.tensor_mul(
                    r_mean[:, 0:W], r_mean[:, 0:W], r_rstd[:, 0:W]),
                    waits=[("act", a)], inc="dve")
                v_mr = v
                pbr = P.op("tensor", lambda e: e.matmul(
                    psBr[:, :], c_orf[:], r_rstd[0:1, 0:W], start=True, stop=True),
                    waits=[("act", a), ("dve", prev["v_xn"].get(f - 1, 0))], inc="pe")
                pbm = P.op("tensor", lambda e: e.matmul(
                    psBm[:, :], c_orf[:], r_mean[0:1, 0:W], start=True, stop=True),
                    waits=[("dve", v_mr)], inc="pe")
                for dt in range(DT):
                    P.op("vector", lambda e, dt=dt, xb=xb: e.tensor_mul(
                        xb[:, dt * W:(dt + 1) * W],
                        xb[:, dt * W:(dt + 1) * W], psBr[:, :]),
                        waits=[("pe", pbr)])
                for dt in range(DT):
                    P.op("vector", lambda e, dt=dt, f=f, xb=xb: e.tensor_sub(
                        xn_dst[:, dt * n_tok + f * W: dt * n_tok + f * W + W],
                        xb[:, dt * W:(dt + 1) * W], psBm[:, :]),
                        waits=[("pe", pbm)],
                        inc="dve" if dt == DT - 1 else None)
                v_xn = K.cnt["dve"]
                prev["v_xn"][f] = v_xn
                v = P.op("vector", lambda e: e.tensor_mul(
                    r_msq[:, 0:W], r_var[:, 0:W], r_rstd[:, 0:W]),
                    waits=[("dve", v_mr)], inc="dve")
                v = P.op("vector", lambda e: e.tensor_mul(
                    r_msq[:, 0:W], r_msq[:, 0:W], r_rstd[:, 0:W]),
                    waits=[("dve", v)], inc="dve")
                a = P.op("scalar", lambda e: e.activation(
                    r_var[:, 0:W], r_msq[:, 0:W], AF.Ln, bias=EPS_RMS2),
                    waits=[("dve", v)], inc="act")
                a = P.op("scalar", lambda e, f=f: e.activation(
                    r2_dst[:, f * W:(f + 1) * W], r_var[:, 0:W], AF.Exp, scale=-0.5),
                    waits=[("act", a)], inc="act")
                a = P.op("scalar", lambda e: e.activation(
                    r_var[:, 0:W], r_msq[:, 0:W], AF.Ln, bias=EPS_LN1),
                    waits=[("act", a)], inc="act")
                a = P.op("scalar", lambda e, f=f: e.activation(
                    r1_dst[:, f * W:(f + 1) * W], r_var[:, 0:W], AF.Exp, scale=-0.5),
                    waits=[("act", a)], inc="act")
                v = P.op("vector", lambda e: e.tensor_copy(
                    r_mean[0:1, 0:1], r_mean[0:1, 0:1]),
                    waits=[("act", a)], inc="dve")
                prev["v_rows"][f] = v
                prev["pe_q2"][f] = pe_q
                spandone[f] = (K.cnt["act"], prev["v_xn"][f])
                issue_load(f + 2)
            return spandone

        def emit_derive_full(P, psB, rows, lnw, dst, spandone=None):
            vprev = 0
            for f in range(4):
                dw = [("dve", vprev)]
                if spandone is not None:
                    act_c, dve_c = spandone[2 * f + 1]
                    dw += [("act", act_c), ("dve", dve_c)]
                pb = P.op("tensor", lambda e, f=f: e.matmul(
                    psB[:, :], c_orf[:], rows[:, f * 512:(f + 1) * 512],
                    start=True, stop=True),
                    waits=dw, inc="pe")
                for dt in range(DT):
                    if lnw is None:
                        P.op("vector", lambda e, dt=dt, f=f: e.tensor_mul(
                            dst[:, dt * TOKF + f * 512: dt * TOKF + f * 512 + 512],
                            xn_full[:, dt * TOKF + f * 512: dt * TOKF + f * 512 + 512],
                            psB[:, :]), waits=[("pe", pb)],
                            inc="dve" if dt == DT - 1 else None)
                    else:
                        P.op("vector", lambda e, dt=dt, f=f: e.tensor_mul(
                            r_xtmp[:, :],
                            xn_full[:, dt * TOKF + f * 512: dt * TOKF + f * 512 + 512],
                            psB[:, :]), waits=[("pe", pb)])
                        P.op("vector", lambda e, dt=dt, f=f: e.tensor_mul(
                            dst[:, dt * TOKF + f * 512: dt * TOKF + f * 512 + 512],
                            r_xtmp[:, :],
                            lnw[:, dt:dt + 1].to_broadcast([128, 512])),
                            inc="dve" if dt == DT - 1 else None)
                vprev = K.cnt["dve"]
            return vprev

        with ExitStack() as pc:
            psS = pc.enter_context(nc.psum_tensor("psS", [1, 256], F32))
            psQ = pc.enter_context(nc.psum_tensor("psQ", [1, 256], F32))
            psBr = pc.enter_context(nc.psum_tensor("psBr", [128, 256], F32))
            psBm = pc.enter_context(nc.psum_tensor("psBm", [128, 256], F32))
            psB2a = pc.enter_context(nc.psum_tensor("psB2a", [128, 512], F32))
            P = Plan(K)
            sd = emit_ln_phase(P, xfT, TOKF, xn_full, r2_all, r1_all,
                               (xbufA, xbufB), sq_sp, psS, psQ, psBr, psBm)
            emit_derive_full(P, psB2a, r2_all, None, xr_full, spandone=sd)
            if debug:
                P.dma("sync", dbg["d_xn"], xn_full[:], "dio",
                      waits=[("dve", K.cnt["dve"]), ("act", K.cnt["act"])])
                P.dma("sync", dbg["d_xr"], xr_full[:], "dio",
                      waits=[("dve", K.cnt["dve"])])
            P.run(nc, "p1")
            if upto == "p1":
                return nc, dbg


        def stream_fm_to_scratch(P, psA, psB, w_dram, w_row_len, w_col0, n_out,
                                 act, dst_dram, dst_row0, src=None):
            evac = {}
            for c in range(n_out // 128):
                wb = wbA if c % 2 == 0 else wbB
                wsem = "dwA" if c % 2 == 0 else "dwB"
                wwaits = [("pe", evac[("pef", c - 2)])] if c >= 2 else []
                wld = P.dma("gpsimd", wb[:],
                            ap2(w_dram, w_col0 + c * 128,
                                [[w_row_len, 128], [128 * w_row_len, DT], [1, 128]]),
                            wsem, waits=wwaits)
                for f in range(4):
                    idx = c * 4 + f
                    ps = psA if idx % 2 == 0 else psB
                    mmw = [(wsem, wld)]
                    if ("ev", idx - 2) in evac:
                        mmw.append(evac[("ev", idx - 2)])
                    srct = xr_full if src is None else src
                    for dt in range(DT):
                        P.op("tensor", lambda e, dt=dt, f=f, ps=ps, wb=wb, srct=srct: e.matmul(
                            ps[:, :], wb[:, dt * 128:(dt + 1) * 128],
                            srct[:, dt * TOKF + f * 512: dt * TOKF + f * 512 + 512],
                            start=(dt == 0), stop=(dt == DT - 1)),
                            waits=mmw if dt == 0 else [],
                            inc="pe" if dt == DT - 1 else None)
                    pe_c = K.cnt["pe"]
                    evac[("pef", c)] = pe_c
                    eb = ebufA if idx % 2 == 0 else ebufB
                    ssem = "sA" if idx % 2 == 0 else "sB"
                    a = P.op("scalar", lambda e, ps=ps, eb=eb: e.activation(
                        eb[:, :], ps[:, :], act),
                        waits=[("pe", pe_c), (ssem, K.cnt[ssem])], inc="act")
                    evac[("ev", idx)] = ("act", a)
                    P.dma("sync",
                          ap2(dst_dram, (dst_row0 + c * 128) * TOKF + f * 512,
                              [[TOKF, 128], [1, 512]]),
                          eb[:, :], ssem, waits=[("act", a)])

        def stream_tm_to_scratch(P, psA, psB, w_dram, w_row_len, w_col0, n_v,
                                 act, dst_dram, src=None):
            evac = {}
            for vc in range(n_v // 256):
                wb = wvA if vc % 2 == 0 else wvB
                wsem = "dwA" if vc % 2 == 0 else "dwB"
                wwaits = [("pe", evac[("pef", vc - 2)])] if vc >= 2 else []
                wld = P.dma("gpsimd", wb[:],
                            ap2(w_dram, w_col0 + vc * 256,
                                [[w_row_len, 128], [128 * w_row_len, DT], [1, 256]]),
                            wsem, waits=wwaits)
                for tt in range(DT):
                    idx = vc * DT + tt
                    ps = psA if idx % 2 == 0 else psB
                    mmw = [(wsem, wld)]
                    if ("ev", idx - 2) in evac:
                        mmw.append(evac[("ev", idx - 2)])
                    srct = xr_full if src is None else src
                    for dt in range(DT):
                        P.op("tensor", lambda e, dt=dt, tt=tt, ps=ps, wb=wb, srct=srct: e.matmul(
                            ps[:, 0:256],
                            srct[:, dt * TOKF + tt * 128: dt * TOKF + tt * 128 + 128],
                            wb[:, dt * 256:(dt + 1) * 256],
                            start=(dt == 0), stop=(dt == DT - 1)),
                            waits=mmw if dt == 0 else [],
                            inc="pe" if dt == DT - 1 else None)
                    pe_c = K.cnt["pe"]
                    evac[("pef", vc)] = pe_c
                    eb = ebufA if idx % 2 == 0 else ebufB
                    ssem = "sA" if idx % 2 == 0 else "sB"
                    a = P.op("scalar", lambda e, ps=ps, eb=eb: e.activation(
                        eb[:, 0:256], ps[:, 0:256], act),
                        waits=[("pe", pe_c), (ssem, K.cnt[ssem])], inc="act")
                    evac[("ev", idx)] = ("act", a)
                    P.dma("sync",
                          ap2(dst_dram, (tt * 128) * D + vc * 256,
                              [[D, 128], [1, 256]]),
                          eb[:, 0:256], ssem, waits=[("act", a)])

        with ExitStack() as pc:
            psA = pc.enter_context(nc.psum_tensor("psA2", [128, 512], F32))
            psB = pc.enter_context(nc.psum_tensor("psB2b", [128, 512], F32))
            P = Plan(K)
            stream_fm_to_scratch(P, psA, psB, wk_t, D, 0, D, AF.Silu, laK, 0)
            P.run(nc, "p2b")
            if upto == "p2b":
                return nc, dbg
            psB2d = pc.enter_context(nc.psum_tensor("psB2d", [128, 512], F32))
            P = Plan(K)
            stream_tm_to_scratch(P, psA, psB, wv_t, D, 0, D, AF.Silu, laV)
            emit_derive_full(P, psB2d, r1_all, c_ln1, xn_full)
            if debug:
                P.dma("sync", dbg["d_x1"], xn_full[:], "dio",
                      waits=[("dve", K.cnt["dve"])])
            P.run(nc, "p2c")
        if upto == "p2c":
            return nc, dbg
        with ExitStack() as pc:
            psA = pc.enter_context(nc.psum_tensor("psA2e", [128, 512], F32))
            psB = pc.enter_context(nc.psum_tensor("psB2e", [128, 512], F32))
            P = Plan(K)
            stream_fm_to_scratch(P, psA, psB, kvw_t, 2 * D, 0, D, AF.Copy, fK, 0,
                                 src=xn_full)
            P.run(nc, "p2e")
            if upto == "p2e":
                return nc, dbg
            P = Plan(K)
            stream_tm_to_scratch(P, psA, psB, kvw_t, 2 * D, D, D, AF.Copy, fV,
                                 src=xn_full)
            P.run(nc, "p2f")

        if upto == "p2f":

            return nc, dbg

        with ExitStack() as pc:
            psS = pc.enter_context(nc.psum_tensor("psS3", [1, 256], F32))
            psQ = pc.enter_context(nc.psum_tensor("psQ3", [1, 256], F32))
            psBr = pc.enter_context(nc.psum_tensor("psBr3", [128, 256], F32))
            psBm = pc.enter_context(nc.psum_tensor("psBm3", [128, 256], F32))
            P = Plan(K)
            emit_ln_phase(P, xoT, TOKO, xn_own, r2o, r1o,
                          (xobufA, xobufB), sq_o, psS, psQ, psBr, psBm)
            if debug:
                P.dma("sync", dbg["d_xno"], xn_own[:], "dio",
                      waits=[("dve", K.cnt["dve"])])
            P.run(nc, "p3a")
            if upto == "p3a":
                return nc, dbg

        def emit_derive_own(P, psB, rows, lnw):
            vprev = 0
            for f in range(2):
                pb = P.op("tensor", lambda e, f=f: e.matmul(
                    psB[:, :], c_orf[:], rows[:, f * 512:(f + 1) * 512],
                    start=True, stop=True), waits=[("dve", vprev)], inc="pe")
                for dt in range(DT):
                    if lnw is None:
                        P.op("vector", lambda e, dt=dt, f=f: e.tensor_mul(
                            xro[:, dt * TOKO + f * 512: dt * TOKO + f * 512 + 512],
                            xn_own[:, dt * TOKO + f * 512: dt * TOKO + f * 512 + 512],
                            psB[:, :]), waits=[("pe", pb)],
                            inc="dve" if dt == DT - 1 else None)
                    else:
                        P.op("vector", lambda e, dt=dt, f=f: e.tensor_mul(
                            r_xtmp[:, :],
                            xn_own[:, dt * TOKO + f * 512: dt * TOKO + f * 512 + 512],
                            psB[:, :]), waits=[("pe", pb)])
                        P.op("vector", lambda e, dt=dt, f=f: e.tensor_mul(
                            xro[:, dt * TOKO + f * 512: dt * TOKO + f * 512 + 512],
                            r_xtmp[:, :],
                            lnw[:, dt:dt + 1].to_broadcast([128, 512])),
                            inc="dve" if dt == DT - 1 else None)
                vprev = K.cnt["dve"]
            return vprev

        def stream_proj_own(P, psA, psB, w_dram, act, dst_sb, src_rdy):
            evac = {}
            pe0 = K.cnt["pe"]
            for c in range(DT):
                wb = wbA if c % 2 == 0 else wbB
                wsem = "dwA" if c % 2 == 0 else "dwB"
                wwaits = [("pe", evac[("pef", c - 2)])] if c >= 2 else [("pe", pe0)]
                wld = P.dma("gpsimd", wb[:],
                            ap2(w_dram, c * 128,
                                [[D, 128], [128 * D, DT], [1, 128]]),
                            wsem, waits=wwaits)
                for f in range(2):
                    idx = c * 2 + f
                    ps = psA if idx % 2 == 0 else psB
                    mmw = [(wsem, wld), ("dve", src_rdy)]
                    if ("ev", idx - 2) in evac:
                        mmw.append(evac[("ev", idx - 2)])
                    for dt in range(DT):
                        P.op("tensor", lambda e, dt=dt, f=f, ps=ps, wb=wb: e.matmul(
                            ps[:, :], wb[:, dt * 128:(dt + 1) * 128],
                            xro[:, dt * TOKO + f * 512: dt * TOKO + f * 512 + 512],
                            start=(dt == 0), stop=(dt == DT - 1)),
                            waits=mmw if dt == 0 else [],
                            inc="pe" if dt == DT - 1 else None)
                    pe_c = K.cnt["pe"]
                    evac[("pef", c)] = pe_c
                    a = P.op("scalar", lambda e, ps=ps, c=c, f=f, dst_sb=dst_sb: e.activation(
                        dst_sb[:, c * TOKO + f * 512: c * TOKO + f * 512 + 512],
                        ps[:, :], act),
                        waits=[("pe", pe_c)], inc="act")
                    evac[("ev", idx)] = ("act", a)

        with ExitStack() as pc:
            psA = pc.enter_context(nc.psum_tensor("psA3", [128, 512], F32))
            psB = pc.enter_context(nc.psum_tensor("psB3", [128, 512], F32))
            psBc = pc.enter_context(nc.psum_tensor("psBc3", [128, 512], F32))
            P = Plan(K)
            vr = emit_derive_own(P, psBc, r2o, None)
            stream_proj_own(P, psA, psB, wq_t, AF.Silu, q_sb, vr)
            stream_proj_own(P, psA, psB, wg_t, AF.Sigmoid, g_sb, vr)
            P.run(nc, "p3b")
            if upto == "p3b":
                return nc, dbg
            P = Plan(K)
            vr = emit_derive_own(P, psBc, r1o, c_ln1)
            stream_proj_own(P, psA, psB, qw_t, AF.Copy, fq_sb, vr)
            if debug:
                ac = K.cnt["act"]
                P.dma("sync", dbg["d_q"], q_sb[:], "dio", waits=[("act", ac)])
                P.dma("sync", dbg["d_g"], g_sb[:], "dio", waits=[("act", ac)])
                P.dma("sync", dbg["d_fq"], fq_sb[:], "dio", waits=[("act", ac)])
            P.run(nc, "p3c")

        if upto == "p3c":

            return nc, dbg

        with ExitStack() as pc:
            stA = pc.enter_context(nc.psum_tensor("stA", [128, 512], F32))
            stB = pc.enter_context(nc.psum_tensor("stB", [128, 512], F32))
            P = Plan(K)
            evac = {}
            ci = 0
            P.op("vector", lambda e: e.memset(st_sb[:], 0), inc="dve")
            for j in range(min(DT, P4A_JMAX)):
                kb = kjA if j % 2 == 0 else kjB
                ksem = "kA" if j % 2 == 0 else "kB"
                kwaits = [("pe", evac[("kf", j - 2)])] if j >= 2 else [
                    ("sA", K.cnt["sA"]), ("sB", K.cnt["sB"])]
                kld = P.dma("sync", kb[:],
                            ap2(laK, j * 128,
                                [[TOKF, 128], [128 * TOKF, DT], [1, 128]]),
                            ksem, waits=kwaits)
                smin = j // 2
                c0 = smin * 128
                chunks = [(c0, 512), (512, 1024)] if c0 < 512 else [(c0, 1024)]
                for (a0, a1) in chunks:
                    ps = stA if ci % 2 == 0 else stB
                    mmw = [(ksem, kld)]
                    if ("ev", ci - 2) in evac:
                        mmw.extend(evac[("ev", ci - 2)])
                    for dt in range(DT):
                        P.op("tensor", lambda e, dt=dt, a0=a0, a1=a1, ps=ps, kb=kb: e.matmul(
                            ps[:, 0:a1 - a0], kb[:, dt * 128:(dt + 1) * 128],
                            q_sb[:, dt * TOKO + a0: dt * TOKO + a1],
                            start=(dt == 0), stop=(dt == DT - 1)),
                            waits=mmw if dt == 0 else [],
                            inc="pe" if dt == DT - 1 else None)
                    pe_c = K.cnt["pe"]
                    evac[("kf", j)] = pe_c
                    if a0 == c0:
                        msk = c_lam0 if j % 2 == 0 else c_lam1
                        w = a1 - a0
                        vv = P.op("vector", lambda e, ps=ps, j=j, a0=a0, w=w, msk=msk: e.tensor_mul(
                            st_sb[:, j * TOKO + a0: j * TOKO + a0 + w],
                            ps[:, 0:w], msk[:, 0:w]),
                            waits=[("pe", pe_c)], inc="dve")
                        evac[("ev", ci)] = [("dve", vv)]
                    else:
                        aa = P.op("scalar", lambda e, ps=ps, j=j, a0=a0, a1=a1: e.activation(
                            st_sb[:, j * TOKO + a0: j * TOKO + a1],
                            ps[:, 0:a1 - a0], AF.Copy),
                            waits=[("pe", pe_c)], inc="act")
                        evac[("ev", ci)] = [("act", aa)]
                    ci += 1
            if debug:
                P.dma("sync", dbg["d_st"], st_sb[:], "dio",
                      waits=[("dve", K.cnt["dve"]), ("act", K.cnt["act"])])
            P.run(nc, "p4a")

        if upto == "p4a":

            return nc, dbg

        with ExitStack() as pc:
            po = [pc.enter_context(nc.psum_tensor(f"po{v}", [128, 1024], F32))
                  for v in range(4)]
            vband = [vbandA, vbandB]
            P = Plan(K)
            evac_done = {}
            pe_after = {}
            lds = {}

            def issue_band(vg):
                if vg >= 8:
                    return
                vsem = "vA" if vg % 2 == 0 else "vB"
                vw = [("pe", pe_after[vg - 2])] if vg >= 2 else [
                    ("sA", K.cnt["sA"]), ("sB", K.cnt["sB"])]
                lds[vg] = P.dma("sync", vband[vg % 2][:, 0:DT * 256],
                                ap2(laV, vg * 256,
                                    [[D, 128], [128 * D, DT], [1, 256]]),
                                vsem, waits=vw)

            issue_band(0)
            issue_band(1)
            for vg in range(8):
                vb = vband[vg % 2]
                vsem = "vA" if vg % 2 == 0 else "vB"
                vld = lds[vg]
                pbase = 2 * (vg % 2)
                first = [[True, True] for _ in range(2)]
                for j in range(DT):
                    smin = j // 2
                    c0 = smin * 128
                    chunks = [(c0, 512), (512, 1024)] if c0 < 512 else [(c0, 1024)]
                    for v in range(2):
                        for (a0, a1) in chunks:
                            bank = 0 if a0 < 512 else 1
                            st_flag = first[v][bank]
                            first[v][bank] = False
                            stop_flag = (j == 7 and bank == 0) or (j == DT - 1 and bank == 1)
                            mmw = [(vsem, vld)]
                            if vg >= 2 and j == 0:
                                mmw.append(("dve", evac_done[vg - 2]))
                            last_mm = (v == 1 and (a0, a1) == chunks[-1] and j == DT - 1)
                            P.op("tensor", lambda e, v=v, j=j, a0=a0, a1=a1, vb=vb,
                                 pbase=pbase, st_flag=st_flag, stop_flag=stop_flag: e.matmul(
                                po[pbase + v][:, a0:a1],
                                vb[:, j * 256 + v * 128: j * 256 + v * 128 + 128],
                                st_sb[:, j * TOKO + a0: j * TOKO + a1],
                                start=st_flag, stop=stop_flag),
                                waits=mmw,
                                inc="pe" if last_mm else None)
                pe_c = K.cnt["pe"]
                pe_after[vg] = pe_c
                issue_band(vg + 2)
                for v in range(2):
                    tile = vg * 2 + v
                    P.op("vector", lambda e, v=v, tile=tile, pbase=pbase: e.tensor_mul(
                        og_sb[:, tile * TOKO:(tile + 1) * TOKO],
                        po[pbase + v][:, :], g_sb[:, tile * TOKO:(tile + 1) * TOKO]),
                        waits=[("pe", pe_c)],
                        inc="dve" if v == 1 else None)
                evac_done[vg] = K.cnt["dve"]
            if debug:
                P.dma("sync", dbg["d_og"], og_sb[:], "dio",
                      waits=[("dve", K.cnt["dve"])])
            P.run(nc, "p4b")
            if upto == "p4b":
                return nc, dbg

        with ExitStack() as pc:
            psA = pc.enter_context(nc.psum_tensor("psA4", [128, 512], F32))
            psB = pc.enter_context(nc.psum_tensor("psB4", [128, 512], F32))
            P = Plan(K)
            evac = {}
            for c in range(DT):
                wb = wpA if c % 2 == 0 else wpB
                wsem = "dwA" if c % 2 == 0 else "dwB"
                wwaits = [("pe", evac[("pef", c - 2)])] if c >= 2 else []
                wld = P.dma("gpsimd", wb[:],
                            ap2(wp_t, c * 128,
                                [[D, 128], [128 * D, DT], [1, 128]]),
                            wsem, waits=wwaits)
                for f in range(2):
                    idx = c * 2 + f
                    ps = psA if idx % 2 == 0 else psB
                    mmw = [(wsem, wld)]
                    if ("ev", idx - 2) in evac:
                        mmw.append(evac[("ev", idx - 2)])
                    for dt in range(DT):
                        P.op("tensor", lambda e, dt=dt, f=f, ps=ps, wb=wb: e.matmul(
                            ps[:, :], wb[:, dt * 128:(dt + 1) * 128],
                            og_sb[:, dt * TOKO + f * 512: dt * TOKO + f * 512 + 512],
                            start=(dt == 0), stop=(dt == DT - 1)),
                            waits=mmw if dt == 0 else [],
                            inc="pe" if dt == DT - 1 else None)
                    pe_c = K.cnt["pe"]
                    evac[("pef", c)] = pe_c
                    a = P.op("scalar", lambda e, ps=ps, c=c, f=f: e.activation(
                        s_sb[:, c * TOKO + f * 512: c * TOKO + f * 512 + 512],
                        ps[:, :], AF.Copy),
                        waits=[("pe", pe_c)], inc="act")
                    evac[("ev", idx)] = ("act", a)
            if debug:
                P.dma("sync", dbg["d_s"], s_sb[:], "dio",
                      waits=[("act", K.cnt["act"])])
            P.run(nc, "p4c")

        if upto == "p4c":

            return nc, dbg

        with ExitStack() as pc:
            sps = [pc.enter_context(nc.psum_tensor(f"sps{i}", [128, 512], F32))
                   for i in range(4)]
            pt_pss = [pc.enter_context(nc.psum_tensor(f"pt_ps{i}", [128, 1024], BF16))
                      for i in range(2)]
            pv_pss = [pc.enter_context(nc.psum_tensor(f"pv_ps{i}", [128, 128], F32))
                      for i in range(2)]
            P = Plan(K)
            pairs = [(s, h) for s in range(NSLOT) for h in range(NH)]
            ksems = ["kA", "kB", "kC"]
            vsems = ["vA", "vB", "vC"]
            D5 = {"pvdone": {}, "pvev": {}, "exp": {}, "pscale": {}, "tev": {},
                  "gci": 0, "spslast": {}, "smm": {}}

            def p5_kload(pi):
                if pi >= len(pairs):
                    return
                s, h = pairs[pi]
                jl = 256 * (s + 1)
                lb = pi % 3
                kw = [("pe", D5["pvdone"].get(pi - 4, 0))]
                if pi == 0:
                    kw = [("sA", K.cnt["sA"]), ("sB", K.cnt["sB"])]
                D5[("kld", pi)] = P.dma(
                    "sync", kbs[lb][:, 0:jl],
                    ap2(fK, (h * 128) * TOKF, [[TOKF, 128], [1, jl]]),
                    ksems[lb], waits=kw)

            def p5_vload(pi):
                if pi >= len(pairs):
                    return
                s, h = pairs[pi]
                njt = 2 * s + 2
                lb = pi % 3
                kw = [("pe", D5["pvdone"].get(pi - 3, 0))]
                if pi == 0:
                    kw = [("sA", K.cnt["sA"]), ("sB", K.cnt["sB"])]
                D5[("vld", pi)] = P.dma(
                    "sync", vbs[lb][:, 0:njt * 128],
                    ap2(fV, h * 128, [[D, 128], [128 * D, njt], [1, 128]]),
                    vsems[lb], waits=kw)

            def p5_smm(pi):
                s, h = pairs[pi]
                jl = 256 * (s + 1)
                lb = pi % 3
                kb = kbs[lb]
                nch = (jl + 511) // 512
                chunks = []
                for c in range(nch):
                    a0, a1 = c * 512, min(jl, c * 512 + 512)
                    gci = D5["gci"]
                    D5["gci"] = gci + 1
                    ps = sps[gci % 4]
                    mmw = [(ksems[lb], D5[("kld", pi)])]
                    if (gci % 4) in D5["spslast"]:
                        mmw.append(D5["spslast"][gci % 4])
                    P.op("tensor", lambda e, s=s, h=h, a0=a0, a1=a1, ps=ps, kb=kb: e.matmul(
                        ps[:, 0:a1 - a0],
                        fq_sb[:, h * TOKO + s * 128: h * TOKO + s * 128 + 128],
                        kb[:, a0:a1], start=True, stop=True),
                        waits=mmw, inc="pe")
                    chunks.append((a0, a1, ps, gci, K.cnt["pe"]))
                D5[("chunks", pi)] = chunks
                D5["smm"][pi] = K.cnt["pe"]

            def hb1(pi):
                s, h = pairs[pi]
                par = pi % 2
                jl = 256 * (s + 1)
                S_sb = S_sbs[par]
                ev_list = []
                for (a0, a1, ps, gci, pe_c) in D5[("chunks", pi)]:
                    w = a1 - a0
                    if a1 == jl:
                        msk = c_ffw512 if w == 512 else c_ffw256
                        vv = P.op("vector", lambda e, ps=ps, a0=a0, a1=a1, w=w,
                                  msk=msk, S_sb=S_sb: e.tensor_add(
                            S_sb[:, a0:a1], ps[:, 0:w], msk[:, 0:w]),
                            waits=[("pe", pe_c),
                                   ("act", D5["exp"].get(pi - 2, 0))],
                            inc="dve")
                        ev = ("dve", vv)
                    else:
                        aa = P.op("scalar", lambda e, ps=ps, a0=a0, a1=a1, w=w,
                                  S_sb=S_sb: e.activation(
                            S_sb[:, a0:a1], ps[:, 0:w], AF.Copy),
                            waits=[("pe", pe_c),
                                   ("act", D5["exp"].get(pi - 2, 0))], inc="act")
                        ev = ("act", aa)
                    ev_list.append(ev)
                    D5["spslast"][gci % 4] = ev
                D5[("evl", pi)] = ev_list
                par = pi % 2
                mrow, brow = r_mrows[par], r_brows[par]
                v = P.op("vector", lambda e, jl=jl, S_sb=S_sb, mrow=mrow: e.reduce_max(
                    mrow[:], S_sb[:, 0:jl], axis=AX.X, negate=True),
                    waits=ev_list + [("act", D5["exp"].get(pi - 2, 0))],
                    inc="dve")
                v = P.op("vector", lambda e, mrow=mrow, brow=brow: e.tensor_scalar_mul(
                    brow[:], mrow[:], SCALE_ATTN),
                    waits=[("dve", v)], inc="dve")
                D5[("brow", pi)] = v

            def st3(pi):
                s, h = pairs[pi]
                par = pi % 2
                jl = 256 * (s + 1)
                njt = 2 * s + 2
                lb = pi % 3
                vsem = vsems[lb]
                vld = D5[("vld", pi)]
                vb = vbs[lb]
                S_sb = S_sbs[par]
                P_sb = P_sbs[par]
                Pt_sb = Pt_sbs[par]
                pt_ps = pt_pss[par]
                pv_ps = pv_pss[par]
                mrow, brow, den, rden = (r_mrows[par], r_brows[par],
                                         r_dens[par], r_rdens[par])
                v = D5[("brow", pi)]
                a = P.op("scalar", lambda e, jl=jl, S_sb=S_sb, P_sb=P_sb,
                         brow=brow, den=den: e.activation(
                    P_sb[:, 0:jl], S_sb[:, 0:jl], AF.Exp,
                    bias=brow[:], scale=SCALE_ATTN, accum_out=den[:]),
                    waits=[("dve", v), ("pe", D5["pvdone"].get(pi - 2, 0)),
                           ("dve", D5["pscale"].get(pi - 2, 0))],
                    inc="act")
                D5["exp"][pi] = a
                v = P.op("vector", lambda e, den=den, rden=rden: e.reciprocal(
                    rden[:], den[:]), waits=[("act", a)], inc="dve")
                v = P.op("vector", lambda e, jl=jl, P_sb=P_sb, rden=rden: e.tensor_mul(
                    P_sb[:, 0:jl], P_sb[:, 0:jl],
                    rden[:].to_broadcast([128, jl])),
                    waits=[("dve", v)], inc="dve")
                D5["pscale"][pi] = v

            def st4(pi):
                s, h = pairs[pi]
                par = pi % 2
                jl = 256 * (s + 1)
                njt = 2 * s + 2
                lb = pi % 3
                vsem = vsems[lb]
                vld = D5[("vld", pi)]
                vb = vbs[lb]
                P_sb = P_sbs[par]
                Pt_sb = Pt_sbs[par]
                pt_ps = pt_pss[par]
                pv_ps = pv_pss[par]
                pscale_rdy = D5["pscale"][pi]
                pv_first = True
                for b0 in range(0, njt, 8):
                    bn = min(8, njt - b0)
                    for k in range(bn):
                        jt = b0 + k
                        mmw = [("act", D5["exp"][pi]), ("dve", pscale_rdy)]
                        if ("tevp", pi, b0 - 8) in D5:
                            mmw.append(D5[("tevp", pi, b0 - 8)])
                        elif b0 == 0 and pi >= 2:
                            mmw.append(D5["tev"].get(pi - 2, ("act", 0)))
                        P.op("tensor", lambda e, jt=jt, k=k, P_sb=P_sb, pt_ps=pt_ps: e.transpose(
                            pt_ps[:, k * 128:(k + 1) * 128],
                            P_sb[:, jt * 128:(jt + 1) * 128], c_id[:]),
                            waits=mmw if k == 0 else [],
                            inc="pe" if k == bn - 1 else None)
                    pe_t = K.cnt["pe"]
                    aa = P.op("scalar", lambda e, b0=b0, bn=bn, Pt_sb=Pt_sb, pt_ps=pt_ps: e.activation(
                        Pt_sb[:, b0 * 128:(b0 + bn) * 128],
                        pt_ps[:, 0:bn * 128], AF.Copy),
                        waits=[("pe", pe_t)], inc="act")
                    D5[("tevp", pi, b0)] = ("act", aa)
                    D5["tev"][pi] = ("act", aa)
                    for k in range(bn):
                        jt = b0 + k
                        mmw = [("act", aa), (vsem, vld),
                               ("dve", D5["pvev"].get(pi - 2, 0))]
                        last = (b0 + bn >= njt and k == bn - 1)
                        P.op("tensor", lambda e, jt=jt, vb=vb, pf=pv_first,
                             last=last, pv_ps=pv_ps, Pt_sb=Pt_sb: e.matmul(
                            pv_ps[:, :], vb[:, jt * 128:(jt + 1) * 128],
                            Pt_sb[:, jt * 128:(jt + 1) * 128],
                            start=pf, stop=last),
                            waits=mmw if k == 0 else [],
                            inc="pe" if last else None)
                        pv_first = False
                pe_pv = K.cnt["pe"]
                D5["pvdone"][pi] = pe_pv
                vv = P.op("vector", lambda e, s=s, h=h, pv_ps=pv_ps: e.tensor_copy(
                    attn_sb[:, h * TOKO + s * 128: h * TOKO + s * 128 + 128],
                    pv_ps[:, :]), waits=[("pe", pe_pv)], inc="dve")
                D5["pvev"][pi] = vv

            NPAIR = len(pairs)
            p5_kload(0)
            p5_kload(1)
            p5_kload(2)
            p5_vload(0)
            p5_vload(1)
            p5_smm(0)
            p5_smm(1)
            hb1(0)
            for pi in range(NPAIR):
                if pi + 1 < NPAIR:
                    hb1(pi + 1)
                st3(pi)
                if pi >= 1:
                    st4(pi - 1)
                p5_kload(pi + 3)
                p5_vload(pi + 2)
                if pi + 2 < NPAIR:
                    p5_smm(pi + 2)
            st4(NPAIR - 1)
            if debug:
                P.dma("sync", dbg["d_attn"], attn_sb[:], "dio",
                      waits=[("dve", K.cnt["dve"])])
            P.run(nc, "p5a")
            if upto == "p5a":
                return nc, dbg

        with ExitStack() as pc:
            psA = pc.enter_context(nc.psum_tensor("psA5", [128, 512], F32))
            psB = pc.enter_context(nc.psum_tensor("psB5", [128, 512], F32))
            P = Plan(K)
            evac = {}
            for c in range(DT):
                wb = owA if c % 2 == 0 else owB
                wsem = "dwA" if c % 2 == 0 else "dwB"
                wwaits = [("pe", evac[("pef", c - 2)])] if c >= 2 else []
                wld = P.dma("gpsimd", wb[:],
                            ap2(ow_t, c * 128,
                                [[D, 128], [128 * D, DT], [1, 128]]),
                            wsem, waits=wwaits)
                for f in range(2):
                    idx = c * 2 + f
                    ps = psA if idx % 2 == 0 else psB
                    mmw = [(wsem, wld)]
                    if ("ev", idx - 2) in evac:
                        mmw.append(evac[("ev", idx - 2)])
                    for dt in range(DT):
                        P.op("tensor", lambda e, dt=dt, f=f, ps=ps, wb=wb: e.matmul(
                            ps[:, :], wb[:, dt * 128:(dt + 1) * 128],
                            attn_sb[:, dt * TOKO + f * 512: dt * TOKO + f * 512 + 512],
                            start=(dt == 0), stop=(dt == DT - 1)),
                            waits=mmw if dt == 0 else [],
                            inc="pe" if dt == DT - 1 else None)
                    pe_c = K.cnt["pe"]
                    evac[("pef", c)] = pe_c
                    vv = P.op("vector", lambda e, ps=ps, c=c, f=f: e.tensor_add(
                        hid_sb[:, c * TOKO + f * 512: c * TOKO + f * 512 + 512],
                        ps[:, :],
                        xn_own[:, c * TOKO + f * 512: c * TOKO + f * 512 + 512]),
                        waits=[("pe", pe_c)], inc="dve")
                    evac[("ev", idx)] = ("dve", vv)
            if debug:
                P.dma("sync", dbg["d_hid"], hid_sb[:], "dio",
                      waits=[("dve", K.cnt["dve"])])
            P.run(nc, "p5b")

        if upto == "p5b":

            return nc, dbg

        with ExitStack() as pc:
            psQ = pc.enter_context(nc.psum_tensor("psQ5", [1, 512], F32))
            psBr = pc.enter_context(nc.psum_tensor("psBr5", [128, 512], F32))
            psG = pc.enter_context(nc.psum_tensor("psG5", [128, NE], F32))
            psT = pc.enter_context(nc.psum_tensor("psT5", [16, 128], F32))
            P = Plan(K)
            for f in range(2):
                mm_done = {}
                for dt in range(DT):
                    sqb = p5sqA if dt % 2 == 0 else p5sqB
                    sw = []
                    if dt >= 2:
                        sw = [("pe", mm_done[dt - 2])]
                    P.op("vector", lambda e, dt=dt, f=f, sqb=sqb: e.tensor_mul(
                        sqb[:, :],
                        hid_sb[:, dt * TOKO + f * 512: dt * TOKO + f * 512 + 512],
                        hid_sb[:, dt * TOKO + f * 512: dt * TOKO + f * 512 + 512]),
                        waits=sw, inc="dve")
                    v_s = K.cnt["dve"]
                    P.op("tensor", lambda e, dt=dt, sqb=sqb: e.matmul(
                        psQ[:, :], c_ocf[:], sqb[:, :],
                        start=(dt == 0), stop=(dt == DT - 1)),
                        waits=[("dve", v_s)], inc="pe")
                    mm_done[dt] = K.cnt["pe"]
                pe_q = K.cnt["pe"]
                v = P.op("vector", lambda e: e.tensor_scalar_mul(
                    r_mean[:], psQ[0:1, :], 1.0 / D),
                    waits=[("pe", pe_q)], inc="dve")
                a = P.op("scalar", lambda e: e.activation(
                    r_var[:], r_mean[:], AF.Ln, bias=EPS_LN2),
                    waits=[("dve", v)], inc="act")
                a = P.op("scalar", lambda e: e.activation(
                    r_rstd[:], r_var[:], AF.Exp, scale=-0.5),
                    waits=[("act", a)], inc="act")
                pb = P.op("tensor", lambda e: e.matmul(
                    psBr[:, :], c_orf[:], r_rstd[0:1, :], start=True, stop=True),
                    waits=[("act", a)], inc="pe")
                for dt in range(DT):
                    sqb = p5sqA if dt % 2 == 0 else p5sqB
                    P.op("vector", lambda e, dt=dt, f=f, sqb=sqb: e.tensor_mul(
                        sqb[:, :],
                        hid_sb[:, dt * TOKO + f * 512: dt * TOKO + f * 512 + 512],
                        psBr[:, :]), waits=[("pe", pb)], inc="dve")
                    v2 = K.cnt["dve"]
                    P.op("vector", lambda e, dt=dt, f=f, sqb=sqb: e.tensor_mul(
                        x2_sb[:, dt * TOKO + f * 512: dt * TOKO + f * 512 + 512],
                        sqb[:, :],
                        c_ln2[:, dt:dt + 1].to_broadcast([128, 512])),
                        waits=[("dve", v2)],
                        inc="dve" if dt == DT - 1 else None)
            v_x2 = K.cnt["dve"]
            v_gate_prev = v_x2
            for it in range(NSLOT):
                for dt in range(DT):
                    mmw = [("dve", v_gate_prev)] if dt == 0 else []
                    P.op("tensor", lambda e, dt=dt, it=it: e.matmul(
                        psG[:, :],
                        x2_sb[:, dt * TOKO + it * 128: dt * TOKO + it * 128 + 128],
                        c_gwb[:, dt * NE:(dt + 1) * NE],
                        start=(dt == 0), stop=(dt == DT - 1)),
                        waits=mmw, inc="pe" if dt == DT - 1 else None)
                pe_g = K.cnt["pe"]
                v = P.op("vector", lambda e: e.tensor_copy(r_L[:], psG[:, :]),
                         waits=[("pe", pe_g)], inc="dve")
                v_gate_prev = v
                v = P.op("vector", lambda e: e.tensor_copy(r_Lw[:], r_L[:]),
                         waits=[("dve", v)], inc="dve")
                for rnd in range(3):
                    v = P.op("vector", lambda e: e.reduce_max(
                        r_es[:], r_Lw[:], axis=AX.X, negate=True),
                        waits=[("dve", v)], inc="dve")
                    if rnd == 0:
                        v = P.op("vector", lambda e: e.tensor_copy(r_m1[:], r_es[:]),
                                 waits=[("dve", v)], inc="dve")
                    v = P.op("vector", lambda e: e.tensor_add(
                        r_ge[:], r_Lw[:], r_es[:].to_broadcast([128, NE])),
                        waits=[("dve", v)], inc="dve")
                    v = P.op("vector", lambda e: e.tensor_single_scalar(
                        r_ge[:], r_ge[:], 0.0, ALU.is_ge),
                        waits=[("dve", v)], inc="dve")
                    v = P.op("vector", lambda e: e.scalar_tensor_tensor(
                        r_Lw[:], r_ge[:], -1e30, r_Lw[:], ALU.mult, ALU.add),
                        waits=[("dve", v)], inc="dve")
                v = P.op("vector", lambda e: e.reduce_max(
                    r_th[:], r_Lw[:], axis=AX.X, negate=True),
                    waits=[("dve", v)], inc="dve")
                v = P.op("vector", lambda e: e.tensor_add(
                    r_ge[:], r_L[:], r_th[:].to_broadcast([128, NE])),
                    waits=[("dve", v)], inc="dve")
                v = P.op("vector", lambda e: e.tensor_single_scalar(
                    r_ge[:], r_ge[:], 0.0, ALU.is_ge),
                    waits=[("dve", v)], inc="dve")
                a = P.op("scalar", lambda e: e.activation(
                    r_wft[:], r_L[:], AF.Exp, bias=r_m1[:]),
                    waits=[("dve", v)], inc="act")
                v = P.op("vector", lambda e: e.tensor_mul(
                    r_wft[:], r_wft[:], r_ge[:]),
                    waits=[("act", a)], inc="dve")
                v = P.op("vector", lambda e: e.reduce_sum(
                    r_wsum[:], r_wft[:], axis=AX.X),
                    waits=[("dve", v)], inc="dve")
                v = P.op("vector", lambda e: e.reciprocal(
                    r_es[:], r_wsum[:]), waits=[("dve", v)], inc="dve")
                v = P.op("vector", lambda e: e.tensor_mul(
                    r_wft[:], r_wft[:], r_es[:].to_broadcast([128, NE])),
                    waits=[("dve", v)], inc="dve")
                pe_t = P.op("tensor", lambda e: e.transpose(
                    psT[:, :], r_wft[:], c_idf[:]),
                    waits=[("dve", v)], inc="pe")
                v = P.op("vector", lambda e, it=it: e.tensor_copy(
                    r_wffm[:, it * 128:(it + 1) * 128], psT[:, :]),
                    waits=[("pe", pe_t)], inc="dve")
            P.dma("sync", wfscr, r_wffm[:], "dio",
                  waits=[("dve", K.cnt["dve"])])
            if debug:
                P.dma("sync", dbg["d_x2"], x2_sb[:], "dio",
                      waits=[("dve", K.cnt["dve"])])
                P.dma("sync", dbg["d_wf"], r_wffm[:], "dio",
                      waits=[("dve", K.cnt["dve"])])
            P.run(nc, "p5c")

        if upto == "p5c":

            return nc, dbg

        with ExitStack() as pc:
            psGa = pc.enter_context(nc.psum_tensor("psGa", [128, 512], F32))
            psGb = pc.enter_context(nc.psum_tensor("psGb", [128, 512], F32))
            psUa = pc.enter_context(nc.psum_tensor("psUa", [128, 512], F32))
            psUb = pc.enter_context(nc.psum_tensor("psUb", [128, 512], F32))
            psDa = pc.enter_context(nc.psum_tensor("psDa", [128, 512], F32))
            psDb = pc.enter_context(nc.psum_tensor("psDb", [128, 512], F32))
            W1ST = {"wi": 0}
            W2ST = {"wi": 0}

            def gu_block(P, w_src_ap_fn, with_wb, wb_rdy, pe_guard):
                dve_fin = 0
                for gt in range(NIT):
                    wi = W1ST["wi"]
                    wb = w1A if wi % 2 == 0 else w1B
                    wsem = "dwA" if wi % 2 == 0 else "dwB"
                    wwaits = [("pe", W1ST.get(("free", wi - 2), 0))]
                    wld = P.dma("gpsimd",
                                ap2(wb, 0, [[4096, 128], [2048, 2], [128, DT], [1, 128]]),
                                w_src_ap_fn(gt), wsem, waits=wwaits)
                    for f in range(2):
                        psGx = psGa if f == 0 else psGb
                        psUx = psUa if f == 0 else psUb
                        mmw = [(wsem, wld)]
                        if wb_rdy:
                            mmw.append(("dve", wb_rdy))
                        pk = ("guev", wi - 1, f)
                        if pk in W1ST:
                            mmw.append(W1ST[pk])
                        for dt in range(DT):
                            P.op("tensor", lambda e, dt=dt, f=f, psGx=psGx, wb=wb: e.matmul(
                                psGx[:, :], wb[:, dt * 128:(dt + 1) * 128],
                                x2_sb[:, dt * TOKO + f * 512: dt * TOKO + f * 512 + 512],
                                start=(dt == 0), stop=(dt == DT - 1)),
                                waits=mmw if dt == 0 else [])
                        for dt in range(DT):
                            P.op("tensor", lambda e, dt=dt, f=f, psUx=psUx, wb=wb: e.matmul(
                                psUx[:, :], wb[:, 2048 + dt * 128:2048 + dt * 128 + 128],
                                x2_sb[:, dt * TOKO + f * 512: dt * TOKO + f * 512 + 512],
                                start=(dt == 0), stop=(dt == DT - 1)),
                                inc="pe" if dt == DT - 1 else None)
                        pe_c = K.cnt["pe"]
                        W1ST[("free", wi)] = pe_c
                        a = P.op("scalar", lambda e, psGx=psGx: e.activation(
                            r_tsil[:, :], psGx[:, :], AF.Silu),
                            waits=[("pe", pe_c)], inc="act")
                        awaits = [("act", a)]
                        if gt == 0 and f == 0:
                            awaits.append(("pe", pe_guard))
                        if with_wb:
                            v = P.op("vector", lambda e, f=f: e.tensor_mul(
                                r_tsil2[:, :], r_tsil[:, :],
                                r_wb[:, f * 512:(f + 1) * 512]),
                                waits=awaits, inc="dve")
                            v = P.op("vector", lambda e, gt=gt, f=f, psUx=psUx: e.tensor_mul(
                                acts_sb[:, gt * TOKO + f * 512: gt * TOKO + f * 512 + 512],
                                r_tsil2[:, :], psUx[:, :]),
                                waits=[("dve", v)], inc="dve")
                        else:
                            v = P.op("vector", lambda e, gt=gt, f=f, psUx=psUx: e.tensor_mul(
                                acts_sb[:, gt * TOKO + f * 512: gt * TOKO + f * 512 + 512],
                                r_tsil[:, :], psUx[:, :]),
                                waits=awaits, inc="dve")
                        W1ST[("guev", wi, f)] = ("dve", v)
                        dve_fin = v
                    W1ST["wi"] = wi + 1
                return dve_fin

            def down_block(P, w_dram, row0_elems, acts_rdy):
                for hc in range(8):
                    wi = W2ST["wi"]
                    wb = w2A if wi % 2 == 0 else w2B
                    wsem = "dwA" if wi % 2 == 0 else "dwB"
                    wwaits = [("pe", W2ST.get(("free", wi - 2), 0))]
                    wld = P.dma("gpsimd", wb[:],
                                ap2(w_dram, row0_elems + hc * 256,
                                    [[D, 128], [128 * D, NIT], [1, 256]]),
                                wsem, waits=wwaits)
                    for o in range(2):
                        for f in range(2):
                            idx = hc * 4 + o * 2 + f
                            ps = psDa if idx % 2 == 0 else psDb
                            mmw = [(wsem, wld), ("dve", acts_rdy)]
                            pk = ("dev", W2ST["blk"], idx - 2)
                            if pk in W2ST:
                                mmw.append(W2ST[pk])
                            for it in range(NIT):
                                P.op("tensor", lambda e, it=it, o=o, ps=ps, wb=wb, f=f: e.matmul(
                                    ps[:, :],
                                    wb[:, it * 256 + o * 128: it * 256 + o * 128 + 128],
                                    acts_sb[:, it * TOKO + f * 512: it * TOKO + f * 512 + 512],
                                    start=(it == 0), stop=(it == NIT - 1)),
                                    waits=mmw if it == 0 else [],
                                    inc="pe" if it == NIT - 1 else None)
                            pe_c = K.cnt["pe"]
                            W2ST[("free", wi)] = pe_c
                            ht = hc * 2 + o
                            v = P.op("vector", lambda e, ps=ps, ht=ht, f=f: e.tensor_add(
                                hid_sb[:, ht * TOKO + f * 512: ht * TOKO + f * 512 + 512],
                                ps[:, :],
                                hid_sb[:, ht * TOKO + f * 512: ht * TOKO + f * 512 + 512]),
                                waits=[("pe", pe_c)], inc="dve")
                            W2ST[("dev", W2ST["blk"], idx)] = ("dve", v)
                    W2ST["wi"] = wi + 1

            P = Plan(K)
            for H in range(2):
                def shsrc(gt, H=H):
                    return ap2(shgu_t, (H * NIT + gt) * 128,
                               [[2 * SHI, 128], [SHI, 2], [128 * 2 * SHI, DT], [1, 128]])
                ardy = gu_block(P, shsrc, False, 0, K.cnt["pe"])
                W2ST["blk"] = H
                down_block(P, shdn_t, (H * NIT * 128) * D, ardy)
            P.run(nc, "p6a")

            if upto == "p6a":

                return nc, dbg

            psW = pc.enter_context(nc.psum_tensor("psW", [128, 512], F32))
            P = Plan(K)
            wrow_guard = 0
            vcp = 0
            for e_i in range(NE):
                wr = P.dma("gpsimd", r_wrow[:],
                           ap2(wfscr, e_i * TOKO, [[TOKO, 1], [1, TOKO]]),
                           "dwA",
                           waits=[("pe", wrow_guard), ("dio", K.cnt["dio"])])
                for f in range(2):
                    pb = P.op("tensor", lambda e, f=f: e.matmul(
                        psW[:, :], c_orb[:], r_wrow[0:1, f * 512:(f + 1) * 512],
                        start=True, stop=True),
                        waits=[("dwA", wr), ("dve", vcp)], inc="pe")
                    vcp = P.op("vector", lambda e, f=f: e.tensor_copy(
                        r_wb[:, f * 512:(f + 1) * 512], psW[:, :]),
                        waits=[("pe", pb)], inc="dve")
                    wrow_guard = pb

                def w1src(gt, e_i=e_i):
                    return ap2(w1_t, e_i * D * 2 * I_FF + gt * 128,
                               [[2 * I_FF, 128], [I_FF, 2],
                                [128 * 2 * I_FF, DT], [1, 128]])
                ardy = gu_block(P, w1src, True, vcp, K.cnt["pe"])
                W2ST["blk"] = 2 + e_i
                down_block(P, w2_t, e_i * I_FF * D, ardy)
                if debug and e_i == 0:
                    P.dma("sync", dbg["d_act0"], acts_sb[:], "dio",
                          waits=[("dve", ardy)])
            P.run(nc, "p6b")
            if upto == "p6b":
                return nc, dbg

        with ExitStack() as pc:
            P = Plan(K)
            for dt in range(DT):
                for f in range(2):
                    P.op("scalar", lambda e, dt=dt, f=f: e.activation(
                        g2_sb[:, dt * TOKO + f * 512: dt * TOKO + f * 512 + 512],
                        hid_sb[:, dt * TOKO + f * 512: dt * TOKO + f * 512 + 512],
                        AF.Silu),
                        inc="act" if (dt == DT - 1 and f == 1) else None)
            a_g2 = K.cnt["act"]
            if debug:
                P.dma("sync", dbg["d_ffo"], hid_sb[:], "dio", waits=[])
            stored = {}
            for sp in range(8):
                xb7 = xo7A if sp % 2 == 0 else xo7B
                lsem = "kA" if sp % 2 == 0 else "kB"
                lw = [("dve", stored[sp - 2])] if sp >= 2 else []
                ld = P.dma("sync", xb7[:],
                           ap2(xoT, sp * 128,
                               [[TOKO, 128], [128 * TOKO, DT], [1, 128]]),
                           lsem, waits=lw)
                for dt in range(DT):
                    v = P.op("vector", lambda e, dt=dt, sp=sp: e.tensor_mul(
                        r_tsil[:, 0:128],
                        s_sb[:, dt * TOKO + sp * 128: dt * TOKO + sp * 128 + 128],
                        g2_sb[:, dt * TOKO + sp * 128: dt * TOKO + sp * 128 + 128]),
                        waits=[("act", a_g2), (lsem, ld),
                               ("sA", K.cnt["sA"])],
                        inc="dve")
                    v = P.op("vector", lambda e, dt=dt, xb7=xb7: e.tensor_add(
                        out7[:, dt * 128:(dt + 1) * 128],
                        xb7[:, dt * 128:(dt + 1) * 128], r_tsil[:, 0:128]),
                        waits=[("dve", v)],
                        inc="dve" if dt == DT - 1 else None)
                stored[sp] = K.cnt["dve"]
                P.dma("sync", ap2(outT, sp * 128,
                                  [[TOKO, 128], [128 * TOKO, DT], [1, 128]]),
                      out7[:], "sA", waits=[("dve", stored[sp])])
            P.op("sync", lambda e: e.nop(),
                 waits=[("sA", K.cnt["sA"])])
            P.run(nc, "p7")

    if upto == "p7":

        return nc, dbg

    return nc, dbg


_NC_CACHE = {}


def _get_nc(debug=False, upto=None):
    key = (debug, upto)
    if key not in _NC_CACHE:
        _NC_CACHE[key] = build(debug, upto)
    return _NC_CACHE[key]


def make_inputs(x_np, weights):
    f32 = np.float32
    wq = np.ascontiguousarray(np.asarray(weights["L0_attn_Wq"]).T, dtype=f32)
    wk = np.ascontiguousarray(np.asarray(weights["L0_attn_Wk"]).T, dtype=f32)
    wv = np.ascontiguousarray(np.asarray(weights["L0_attn_Wv"]).T, dtype=f32)
    wg = np.ascontiguousarray(np.asarray(weights["L0_attn_Wg"]).T, dtype=f32)
    wp = np.ascontiguousarray(np.asarray(weights["L0_attn_Wp"]).T, dtype=f32)
    qw = np.asarray(weights["L0_ffn_q_w"])
    qsel = np.concatenate([np.arange(h * 192, h * 192 + 128) for h in range(NH)])
    qw_nope = np.ascontiguousarray(qw[qsel].T, dtype=f32)
    kvw = np.ascontiguousarray(np.asarray(weights["L0_ffn_kv_w"])[:2 * D].T, dtype=f32)
    ow = np.ascontiguousarray(np.asarray(weights["L0_ffn_o_w"]).T, dtype=f32)
    gw = np.ascontiguousarray(np.asarray(weights["L0_ffn_gate_w"]).T, dtype=f32)
    w1 = np.ascontiguousarray(np.transpose(np.asarray(weights["L0_ffn_w1"]), (0, 2, 1)), dtype=f32)
    w2 = np.ascontiguousarray(np.transpose(np.asarray(weights["L0_ffn_w2"]), (0, 2, 1)), dtype=f32)
    shgu = np.ascontiguousarray(np.asarray(weights["L0_ffn_sh_gu_w"]).T, dtype=f32)
    shdn = np.ascontiguousarray(np.asarray(weights["L0_ffn_sh_down_w"]).T, dtype=f32)
    ln1 = np.ascontiguousarray(
        np.asarray(weights["L0_ffn_ln1_w"], dtype=f32).reshape(DT, 128).T)
    ln2 = np.ascontiguousarray(
        np.asarray(weights["L0_ffn_ln2_w"], dtype=f32).reshape(DT, 128).T)

    tri_ji = np.triu(np.ones((128, 128), dtype=f32))
    tril_add = np.where(np.tril(np.ones((128, 128))), 0.0, -1e9).astype(f32)
    ones128 = np.ones((128, 128), dtype=f32)
    zeros128 = np.zeros((128, 128), dtype=f32)
    fullneg = np.full((128, 128), -1e9, dtype=f32)
    shared = dict(
        wq_t=wq, wk_t=wk, wv_t=wv, wg_t=wg, wp_t=wp, qw_t=qw_nope,
        kvw_t=kvw, ow_t=ow, gw_t=gw, w1_t=w1, w2_t=w2,
        shgu_t=shgu, shdn_t=shdn, ln1_r=ln1, ln2_r=ln2,
        identb=np.eye(128, dtype=bfloat16),
        identf=np.eye(128, dtype=np.float32),
        ones_colf=np.ones((128, 1), dtype=f32),
        ones_colb=np.ones((128, 1), dtype=bfloat16),
        ones_rowb=np.ones((1, 128), dtype=bfloat16),
        ones_rowf=np.ones((1, 128), dtype=f32),
        epsv=np.broadcast_to(np.array([EPS_LN, EPS_RMS2, EPS_LN1, EPS_LN2],
                                      dtype=f32), (128, 4)).copy(),
    )
    in_maps = []
    idxs = []
    for c in range(8):
        b, p = c // 2, c % 2
        idx = np.concatenate([np.arange((2 * s + p) * 128, (2 * s + p) * 128 + 128)
                              for s in range(NSLOT)])
        idxs.append((b, idx))
        m = dict(shared)
        m["xfT"] = np.ascontiguousarray(x_np[b].T, dtype=f32)
        m["xoT"] = np.ascontiguousarray(x_np[b][idx].T, dtype=f32)
        if p == 0:
            b0m, b1m = tril_add, fullneg
            lm0, lm1 = tri_ji, zeros128
        else:
            b0m, b1m = zeros128, tril_add
            lm0, lm1 = ones128, tri_ji
        m["la_m0w"] = np.concatenate(
            [lm0, np.ones((128, 384), dtype=f32)], axis=1).astype(bfloat16)
        m["la_m1w"] = np.concatenate(
            [lm1, np.ones((128, 384), dtype=f32)], axis=1).astype(bfloat16)
        m["ff_w512"] = np.concatenate(
            [np.zeros((128, 256), dtype=f32), b0m, b1m], axis=1).astype(bfloat16)
        m["ff_w256"] = np.concatenate([b0m, b1m], axis=1).astype(bfloat16)
        in_maps.append(m)
    return in_maps, idxs


def kernel(**inputs):
    _ = inputs["L0_ffn_positions"]
    x = np.asarray(inputs["L0_attn_x"], dtype=np.float32)
    nc, _dbg = _get_nc(False)
    in_maps, idxs = make_inputs(x, inputs)
    res = run_bass_kernel_spmd(nc, in_maps, core_ids=list(range(8)))
    out = np.empty((4, TOKF, D), dtype=np.float32)
    for c in range(8):
        b, idx = idxs[c]
        out[b][idx] = np.asarray(res.results[c]["outT"]).T
    return out
